# version 20
# baseline (speedup 1.0000x reference)
import os
import numpy as np
import ml_dtypes
import concourse.bass as bass
import concourse.mybir as mybir
from concourse.bass_utils import run_bass_kernel_spmd
from concourse.alu_op_type import AluOpType

DBG = os.environ.get("KDBG") == "1"
F32 = mybir.dt.float32
BF16 = mybir.dt.bfloat16
AF = mybir.ActivationFunctionType
AX = mybir.AxisListType.X
OP = AluOpType

NT = 16
EPS = 1e-5
C_EXPB, C_EPS, C_MAG, C_NMAG, C_ZERO = 0, 1, 2, 3, 6


class P:

    def __init__(self, nc, es):
        self.nc = nc
        self.es = es
        self.ops = {e: [] for e in ("t", "v", "s", "g", "y")}
        self.sems = {}
        self.cnt = {}

    def sem(self, name):
        self.sems[name] = self.es.enter_context(self.nc.semaphore(name))
        self.cnt[name] = 0

    def emit(self, eng, fn, waits=(), inc=None, amt=1):
        selfwait = None
        if eng == "v":
            selfwait = ("sv", self.cnt["sv"])
            if inc is None:
                inc = "sv"
        elif eng == "s":
            selfwait = ("ss", self.cnt["ss"])
            if inc is None:
                inc = "ss"
        tok = None
        if inc is not None:
            self.cnt[inc] += amt
            tok = (inc, self.cnt[inc])
        wl = [w for w in waits if w is not None]
        if selfwait is not None and selfwait[1] > 0:
            wl.append(selfwait)
        self.ops[eng].append((wl, fn, (inc, amt) if inc else None))
        return tok

    def tok(self, sem):
        return (sem, self.cnt[sem])

    def bind(self, block):
        sems = self.sems

        def runner(ops):
            def _(eng):
                hiw = {}
                for waits, fn, inc in ops:
                    for (s, v) in waits:
                        if hiw.get(s, -1) < v:
                            eng.wait_ge(sems[s], v)
                            hiw[s] = v
                    ins = fn(eng)
                    if inc is not None:
                        if ins is None:
                            eng.sem_inc(sems[inc[0]], inc[1])
                        else:
                            ins.then_inc(sems[inc[0]], inc[1])
            return _

        block.tensor(runner(self.ops["t"]))
        block.vector(runner(self.ops["v"]))
        block.scalar(runner(self.ops["s"]))
        block.gpsimd(runner(self.ops["g"]))
        block.sync(runner(self.ops["y"]))


def build_nc():
    from contextlib import ExitStack
    nc = bass.Bass("TRN2", target_bir_lowering=False, detect_race_conditions=False)

    x_in = nc.dram_tensor("x_in", [2048, 1024], F32, kind="ExternalInput")
    wq2 = nc.dram_tensor("wq2", [1024, 1024], F32, kind="ExternalInput")
    wk2 = nc.dram_tensor("wk2", [1024, 1024], F32, kind="ExternalInput")
    wv2 = nc.dram_tensor("wv2", [1024, 1024], F32, kind="ExternalInput")
    pw2 = nc.dram_tensor("pw2", [1024, 1024], F32, kind="ExternalInput")
    fw2 = nc.dram_tensor("fw2", [1024, 1024], F32, kind="ExternalInput")
    biasP_in = nc.dram_tensor("biasP", [128, 1024], F32, kind="ExternalInput")
    biasF_in = nc.dram_tensor("biasF", [128, 1024], F32, kind="ExternalInput")
    maskT_in = nc.dram_tensor("maskT", [128, 2048], mybir.dt.float8e5, kind="ExternalInput")
    eperm_in = nc.dram_tensor("eperm", [64, 256], F32, kind="ExternalInput")
    ident_in = nc.dram_tensor("ident", [128, 128], F32, kind="ExternalInput")
    identb_in = nc.dram_tensor("identb", [128, 128], BF16, kind="ExternalInput")
    onest_in = nc.dram_tensor("onest", [128, 128], F32, kind="ExternalInput")
    cvec_in = nc.dram_tensor("cvec", [128, 8], F32, kind="ExternalInput")
    out = nc.dram_tensor("out", [2048, 1024], F32, kind="ExternalOutput")
    dbg = {}
    if DBG:
        for nm, shp in (("d_xnT", [128, 2048]), ("d_x2", [128, 16384]),
                        ("d_wqz", [128, 8192])):
            dbg[nm] = nc.dram_tensor(nm, shp, BF16 if nm == "d_wqz" else F32,
                                     kind="ExternalOutput")
    ccin = nc.dram_tensor("ccin", [1, 1], F32)
    ccout = nc.dram_tensor("ccout", [1, 1], F32)

    es = ExitStack()
    S = lambda *a: es.enter_context(nc.sbuf_tensor(*a))

    x_res = S("x_res", [128, NT * 1024], F32)
    xnT = S("xnT", [128, 8 * 256], F32)
    qtk_hi = S("qtk_hi", [128, 2 * 1024], BF16)
    qtk_lo = S("qtk_lo", [128, 2 * 1024], BF16)
    ktk_hi = S("ktk_hi", [128, 2 * 1024], BF16)
    ktk_lo = S("ktk_lo", [128, 2 * 1024], BF16)
    vtok = S("vtok", [128, 2 * 1024], F32)
    qp_hi = S("qp_hi", [128, 2048], BF16)
    qp_lo = S("qp_lo", [128, 2048], BF16)
    kp = S("kp", [128, 2048], BF16)
    rr2 = S("rr2", [65, 512], F32)
    vgo = S("vgo", [128, 16 * 65], F32)
    ptA = S("ptA", [128, 4 * 512], F32)
    pt = [ptA[:, 512 * k:512 * k + 512] for k in range(4)]
    xn_sc = ptA[:, 1024:2048]
    ot = S("ot", [64, 2048], F32)
    rb = S("rb", [128, 512], F32)
    aot = S("aot", [128, 8 * 256], F32)
    z0 = S("z0", [128, 1024], F32)
    wb = [S(f"wb{k}", [128, 1024], F32) for k in range(2)]
    wqz = S("wqz", [128, 8 * 1024], BF16)
    xq = [S(f"xq{k}", [128, 1024], BF16) for k in range(2)]
    xqT = S("xqT", [128, 1024], BF16)
    qa = S("qa", [128, 1024], F32)
    zn = S("zn", [128, 1024], F32)
    qc = S("qc", [128, 1024], F32)
    qd = ptA[:, 0:512].bitcast(BF16)
    qm = S("qm", [128, 1024], mybir.dt.uint8)
    biasP = S("biasP_s", [128, 1024], F32)
    biasF = S("biasF_s", [128, 1024], F32)
    maskT = S("maskT_s", [128, 2048], mybir.dt.float8e5)
    eperm = S("eperm_s", [64, 256], F32)
    ident = S("ident_s", [128, 128], F32)
    identb = S("identb_s", [128, 128], BF16)
    onest = S("onest_s", [128, 128], F32)
    cvec = S("cvec_s", [128, 8], F32)
    bnst = S("bnst", [128, 12], F32)
    mv = S("mv", [128, 2], F32)
    t_a = S("t_a", [128, 1], F32)
    t_b = S("t_b", [128, 1], F32)
    rstd = S("rstd", [128, 1], F32)
    nmr = S("nmr", [128, 1], F32)
    r3 = S("r3", [128, NT], F32)
    nm3 = S("nm3", [128, NT], F32)
    mxs = S("mxs", [128, NT], F32)
    mns = S("mns", [128, NT], F32)
    ramax = S("ramax", [128, NT], F32)
    wA = S("wA", [128, 8], F32)
    wred = S("wred", [128, 1], F32)
    sc1 = S("sc1", [128, 4], F32)
    svx = S("svx", [128, 1], F32)
    svw = S("svw", [128, 1], F32)
    ixw = S("ixw", [128, 1], F32)

    pb = [es.enter_context(nc.psum_tensor(f"pb{k}", [128, 512], F32)) for k in range(8)]

    p = P(nc, es)
    for s in ("sy", "sw0", "sw1", "st", "sv", "ss", "sg16", "sgc", "scc", "sout"):
        p.sem(s)

    SW = ("sw0", "sw1")
    wb_free = [None, None]
    wslot = [0]

    def load_w(dram, c):
        k = wslot[0] % 2
        wslot[0] += 1
        tok = None
        for q in range(2):
            src = bass.AP(dram, (128 * c + 64 * q) * 1024, [[1024, 64], [1, 1024]])
            tok = p.emit("y", lambda y, s=src, k=k, q=q: y.dma_start(
                out=wb[k][64 * q:64 * q + 64, :], in_=s),
                waits=[wb_free[k]] if q == 0 else [], inc=SW[k], amt=16)
        return k, tok

    def ecopy(e, dst, srcv):
        return e.tensor_copy(dst, srcv) if hasattr(e, "tensor_copy") else e.copy(dst, srcv)

    bank_free = [None] * 8
    ln_guard = [None]
    q_guard = [None]
    xq_guard = [None]
    pt_guard = {0: None, 1: None, 2: None, 3: None}
    ot_guard = [None]
    rb_guard = [None]
    rr_guard = [None]
    vg_guard = [None]
    zn_guard = [None]
    z0_guard = [None]

    dbw = {}

    def dump(nm, src_ap, waits):
        if DBG:
            dbw[nm] = p.emit("y", lambda y, nm=nm, s=src_ap: y.dma_start(
                out=dbg[nm].ap(), in_=s), waits=waits, inc="sout", amt=16)

    def dgd(nm):
        return dbw.get(nm)

    p.emit("y", lambda y: y.dma_start(
        out=bass.AP(x_res, 0, [[NT * 1024, 128], [1024, NT], [1, 1024]]),
        in_=bass.AP(x_in, 0, [[1024, 128], [131072, NT], [1, 1024]])), inc="sy", amt=16)
    for dst, src in ((biasP, biasP_in), (biasF, biasF_in), (maskT, maskT_in),
                     (eperm, eperm_in), (ident, ident_in), (identb, identb_in),
                     (onest, onest_in), (cvec, cvec_in)):
        p.emit("y", lambda y, d=dst, s=src: y.dma_start(out=d[:, :], in_=s.ap()),
               inc="sy", amt=16)
    tok_consts = p.tok("sy")
    tok_x = tok_consts

    tok_vones = p.emit("g", lambda g: g.memset(
        bass.AP(vgo, 64, [[1040, 128], [65, 16]]), 1.0), inc="sgc")

    for c in range(8):
        k, tokl = load_w(fw2, c)
        tokr = p.emit("v", lambda v, k=k, c=c: v.tensor_reduce(
            wA[:, c:c + 1], wb[k][:, :], AX, OP.max, apply_absolute_value=True),
            waits=[tokl], inc="sv")
        wb_free[k] = tokr
    tok = p.emit("v", lambda v: v.tensor_reduce(wred[:, :], wA[:, :], AX, OP.max), inc="sv")
    tok = p.emit("t", lambda t: t.transpose(pb[6][0:1, 0:128], wred[:, :], ident[:, :]),
                 waits=[tok, tok_consts, bank_free[6]], inc="st")
    tok = p.emit("v", lambda v: v.tensor_reduce(sc1[0:1, 2:3], pb[6][0:1, 0:128], AX, OP.max),
                 waits=[tok], inc="sv")
    bank_free[6] = tok
    tok = p.emit("v", lambda v: v.tensor_scalar(sc1[0:1, 2:3], sc1[0:1, 2:3], 1e-12, None,
                                                OP.max), waits=[tok], inc="sv")
    tok = p.emit("v", lambda v: v.reciprocal(sc1[0:1, 3:4], sc1[0:1, 2:3]),
                 waits=[tok], inc="sv")
    tok = p.emit("v", lambda v: v.tensor_scalar(sc1[0:1, 2:3], sc1[0:1, 3:4], 448.0, 448.0,
                                                OP.mult, OP.min), waits=[tok], inc="sv")
    tok_sw = p.emit("v", lambda v: v.reciprocal(sc1[0:1, 3:4], sc1[0:1, 2:3]),
                    waits=[tok], inc="sv")
    tok = p.emit("t", lambda t: t.matmul(pb[6][:, 0:1], onest[0:1, 0:128], sc1[0:1, 2:3],
                                         start=True, stop=True),
                 waits=[tok_sw, bank_free[6]], inc="st")
    tok_swb = p.emit("v", lambda v: v.tensor_copy(svw[:, :], pb[6][:, 0:1]),
                     waits=[tok], inc="sv")
    bank_free[6] = tok_swb

    def quant(src_ap, svec_col, dst_ap, waits, vbank=6, mid=None):
        t1 = p.emit("v", lambda v: v.tensor_scalar(qa[:, :], src_ap, svec_col, 448.0,
                                                   OP.mult, OP.min),
                    waits=list(waits) + [q_guard[0]], inc="sv")
        t2 = p.emit("v", lambda v: v.tensor_scalar(qa[:, :], qa[:, :], -448.0, None, OP.max),
                    waits=[t1], inc="sv")
        if mid is not None:
            mid()
        if True:
            vb2 = [pb[vbank][:, :], pb[vbank + 1][:, :]]
            t3a = p.emit("s", lambda s: s.activation(vb2[0], qa[:, 0:512], AF.Identity,
                                                     bias=cvec[:, C_ZERO:C_ZERO + 1],
                                                     scale=1048577.0),
                         waits=[t2, bank_free[vbank]], inc="ss")
            t3 = p.emit("s", lambda s: s.activation(vb2[1], qa[:, 512:1024], AF.Identity,
                                                    bias=cvec[:, C_ZERO:C_ZERO + 1],
                                                    scale=1048577.0),
                        waits=[bank_free[vbank + 1]], inc="ss")
            t4 = None
            for hf in range(2):
                t4 = p.emit("v", lambda v, hf=hf: v.tensor_tensor(
                    qc[:, 512 * hf:512 * hf + 512], vb2[hf], qa[:, 512 * hf:512 * hf + 512],
                    OP.subtract), waits=[t3a, t3], inc="sv")
            t5 = None
            for hf in range(2):
                t5 = p.emit("v", lambda v, hf=hf, dst_ap=dst_ap: v.tensor_tensor(
                    dst_ap[:, 512 * hf:512 * hf + 512], vb2[hf],
                    qc[:, 512 * hf:512 * hf + 512], OP.subtract),
                    waits=[t4, xq_guard[0]], inc="sv")
            bank_free[vbank] = t5
            bank_free[vbank + 1] = t5
        t6 = p.emit("s", lambda s: s.activation(qc[:, :], qa[:, :], AF.Identity,
                                                bias=cvec[:, C_MAG:C_MAG + 1], scale=1.0),
                    waits=[t2, t5], inc="ss")
        t7 = p.emit("s", lambda s: s.activation(qc[:, :], qc[:, :], AF.Identity,
                                                bias=cvec[:, C_NMAG:C_NMAG + 1], scale=1.0),
                    waits=[t6], inc="ss")
        t8 = p.emit("s", lambda s: s.activation(qd[:, :], qa[:, :], AF.Abs,
                                                bias=cvec[:, C_ZERO:C_ZERO + 1], scale=1.0),
                    waits=[t2, pt_guard[0]], inc="ss")
        t9 = p.emit("v", lambda v: v.tensor_scalar(qm[:, :], qd[:, :], 0.015625, None,
                                                   OP.is_lt),
                    waits=[t8], inc="sv")
        pt_guard[0] = t9
        t10 = p.emit("v", lambda v, dst_ap=dst_ap: v.copy_predicated(
            dst_ap, qm[:, :], qc[:, :]), waits=[t5, t7, t9], inc="sv")
        q_guard[0] = t8
        return t10, t1

    def layernorm_stats(src_ap, waits):
        w1 = p.emit("v", lambda v: v.bn_stats(bnst[:, 0:6], src_ap[:, 0:512]),
                    waits=list(waits), inc="sv")
        p.emit("v", lambda v: v.bn_stats(bnst[:, 6:12], src_ap[:, 512:1024]), inc="sv")
        w3 = p.emit("v", lambda v: v.bn_aggr(mv[:, :], bnst[:, :]),
                    waits=[ln_guard[0]], inc="sv")
        w4 = p.emit("s", lambda s: s.activation(t_a[:, :], mv[:, 1:2], AF.Sqrt,
                                                bias=cvec[:, C_EPS:C_EPS + 1], scale=1.0),
                    waits=[w3, tok_consts], inc="ss")
        ln_guard[0] = w4
        w5 = p.emit("v", lambda v: v.reciprocal(rstd[:, :], t_a[:, :]), waits=[w4], inc="sv")
        p.emit("v", lambda v: v.tensor_scalar(t_b[:, :], mv[:, 1:2], EPS, None, OP.add),
               inc="sv")
        p.emit("v", lambda v: v.tensor_tensor(t_a[:, :], rstd[:, :], rstd[:, :], OP.mult),
               inc="sv")
        p.emit("v", lambda v: v.tensor_tensor(t_a[:, :], t_a[:, :], t_b[:, :], OP.mult),
               inc="sv")
        p.emit("v", lambda v: v.tensor_scalar(t_a[:, :], t_a[:, :], -0.5, 1.5,
                                              OP.mult, OP.add), inc="sv")
        p.emit("v", lambda v: v.tensor_tensor(rstd[:, :], rstd[:, :], t_a[:, :], OP.mult),
               inc="sv")
        p.emit("v", lambda v: v.tensor_tensor(nmr[:, :], mv[:, 0:1], rstd[:, :], OP.mult),
               inc="sv")
        w12 = p.emit("v", lambda v: v.tensor_scalar(nmr[:, :], nmr[:, :], -1.0, None,
                                                    OP.mult), inc="sv")
        return w12

    def emit_ln1_pre(h):
        i0 = 2 * h
        w0 = layernorm_stats(x_res[:, 1024 * i0:1024 * (i0 + 1)], [tok_x])
        txn0 = p.emit("s", lambda s, i0=i0: s.activation(
            z0[:, :], x_res[:, 1024 * i0:1024 * (i0 + 1)], AF.Identity,
            bias=nmr[:, 0:1], scale=rstd[:, 0:1]),
            waits=[w0, z0_guard[0]], inc="ss")
        ln_guard[0] = txn0
        w1 = layernorm_stats(x_res[:, 1024 * (i0 + 1):1024 * (i0 + 2)], [tok_x])
        txn1 = p.emit("s", lambda s, i0=i0: s.activation(
            zn[:, :], x_res[:, 1024 * (i0 + 1):1024 * (i0 + 2)], AF.Identity,
            bias=nmr[:, 0:1], scale=rstd[:, 0:1]),
            waits=[w1, zn_guard[0]], inc="ss")
        ln_guard[0] = txn1
        return txn0, txn1

    def emit_ln1_post(h, pre, banks=(4, 5), evac_waits=()):
        txn0, txn1 = pre
        for il in range(2):
            txn = txn0 if il == 0 else txn1
            srcb = z0 if il == 0 else zn
            for g in range(2):
                b = banks[g]
                ttr = None
                for r in range(4):
                    c = 4 * g + r
                    ttr = p.emit("t", lambda t, c=c, b=b, r=r, srcb=srcb: t.transpose(
                        pb[b][:, 128 * r:128 * r + 128],
                        srcb[:, 128 * c:128 * c + 128], ident[:, :]),
                        waits=[txn, bank_free[b]] if r == 0 else [],
                        inc="st" if r == 3 else None)
                eng, sm = ("v", "sv") if g == 0 else ("s", "ss")
                tev = p.emit(eng, lambda e, b=b, g=g, il=il: ecopy(e,
                    bass.AP(xnT, 1024 * g + 128 * il, [[2048, 128], [256, 4], [1, 128]]),
                    pb[b][:, :]),
                    waits=[ttr, dgd("d_xnT")] + list(evac_waits), inc=sm)
                bank_free[b] = tev
                if il == 0:
                    z0_guard[0] = ttr
                else:
                    zn_guard[0] = ttr
        return p.tok("sv"), p.tok("ss")

    xnT_toks = emit_ln1_post(0, emit_ln1_pre(0))

    for h in range(8):
        tok_xnT, tok_xnT_s = xnT_toks

        qk_done = []
        for c in range(8):
            kq, tlq = load_w(wq2, c)
            kk, tlk = load_w(wk2, c)
            for mat in range(2):
                kslot = kq if mat == 0 else kk
                tl = tlq if mat == 0 else tlk
                for mt in range(2):
                    for oh in range(2):
                        b = 4 * mat + 2 * mt + oh
                        w_s = [tl, tok_xnT, tok_xnT_s]
                        if c == 0:
                            w_s.append(bank_free[b])
                        last = (mt == 1 and oh == 1)
                        tm = p.emit("t", lambda t, c=c, mt=mt, oh=oh, b=b, ks=kslot:
                                    t.matmul(
                            pb[b][:, :],
                            xnT[:, 256 * c + 128 * mt:256 * c + 128 * mt + 128],
                            wb[ks][:, 512 * oh:512 * oh + 512],
                            start=(c == 0), stop=(c == 7)),
                            waits=w_s, inc="st" if last or c == 7 else None)
                        if c == 7:
                            qk_done.append((b, tm, mat, mt, oh))
                wb_free[kslot] = p.tok("st")
        qk_ev = {}
        for (b, tmm, mat, mt, oh) in qk_done:
            dh = qtk_hi if mat == 0 else ktk_hi
            dl = qtk_lo if mat == 0 else ktk_lo
            sl = slice(1024 * mt + 512 * oh, 1024 * mt + 512 * oh + 512)
            eng, sm = ("v", "sv") if b % 2 == 0 else ("s", "ss")
            th = p.emit(eng, lambda e, b=b, dh=dh, sl=sl: ecopy(e, dh[:, sl], pb[b][:, :]),
                        waits=[tmm], inc=sm)
            tev = p.emit("v", lambda v, b=b, dh=dh, dl=dl, sl=sl: v.tensor_tensor(
                dl[:, sl], pb[b][:, :], dh[:, sl], OP.subtract),
                waits=[th], inc="sv")
            bank_free[b] = tev
            qk_ev[(mat, mt, oh)] = (th, tev)
        tok_qk_ev = p.tok("sv")
        tok_qk_es = p.tok("ss")

        v_done = []
        for c in range(8):
            kv, tlv = load_w(wv2, c)
            for mt in range(2):
                for oh in range(2):
                    b = 2 * mt + oh
                    w_s = [tlv]
                    if c == 0:
                        w_s.append(bank_free[b])
                    last = (mt == 1 and oh == 1)
                    tm = p.emit("t", lambda t, c=c, mt=mt, oh=oh, b=b, kv=kv: t.matmul(
                        pb[b][:, :],
                        xnT[:, 256 * c + 128 * mt:256 * c + 128 * mt + 128],
                        wb[kv][:, 512 * oh:512 * oh + 512],
                        start=(c == 0), stop=(c == 7)),
                        waits=w_s, inc="st" if last or c == 7 else None)
                    if c == 7:
                        v_done.append((b, tm, mt, oh))
            wb_free[kv] = p.tok("st")
        tok_v_mms = p.tok("st")
        for (b, tmm, mt, oh) in v_done:
            eng, sm = ("v", "sv") if b % 2 == 0 else ("s", "ss")
            tev = p.emit(eng, lambda e, b=b, mt=mt, oh=oh: ecopy(e,
                vtok[:, 1024 * mt + 512 * oh:1024 * mt + 512 * oh + 512], pb[b][:, :]),
                waits=[tmm, vg_guard[0]], inc=sm)
            bank_free[b] = tev
        tok_v_ev = p.tok("sv")
        tok_v_es = p.tok("ss")

        if h < 7:
            ln1_pre = emit_ln1_pre(h + 1)

        for gl in range(2):
            gsp = gl
            gw = [tok_v_ev, tok_v_es, tok_vones, bank_free[2], bank_free[3]]
            for j in range(16):
                vg_guard[0] = p.emit("g", lambda g, j=j, gl=gl: g.dma_start(
                    out=bass.AP(vgo, 65 * j, [[1040, 128], [1, 64]]),
                    in_=bass.AP(vtok, (8 * j) * 2048 + 1024 * gl,
                                [[2048, 8], [1, 1024]])),
                    waits=gw if j == 0 else [], inc="sg16", amt=16)
            tok_vg = vg_guard[0]
            for g in range(4):
                bA = 4 + 2 * (g % 2)
                bB = bA + 1
                bA16 = pb[bA][:, :].bitcast(BF16)
                bB16 = pb[bB][:, :].bitcast(BF16)
                ttrA = ttrB = None
                for r in range(4):
                    i = 4 * g + r
                    ssl = slice(1024 * gl + 64 * i, 1024 * gl + 64 * i + 64)
                    wtA = (list(qk_ev[(0, gl, g // 2)]) + [bank_free[bA]]) if r == 0 else []
                    p.emit("t", lambda t, ssl=ssl, bA16=bA16, r=r: t.transpose(
                        bA16[0:64, 128 * r:128 * r + 128], qtk_hi[:, ssl], identb[:, :]),
                        waits=wtA)
                    p.emit("t", lambda t, ssl=ssl, bA16=bA16, r=r: t.transpose(
                        bA16[64:128, 128 * r:128 * r + 128], qtk_hi[:, ssl], identb[:, :]))
                    p.emit("t", lambda t, ssl=ssl, bA16=bA16, r=r: t.transpose(
                        bA16[0:64, 512 + 128 * r:512 + 128 * r + 128], qtk_lo[:, ssl],
                        identb[:, :]))
                    ttrA = p.emit("t", lambda t, ssl=ssl, bA16=bA16, r=r: t.transpose(
                        bA16[64:128, 512 + 128 * r:512 + 128 * r + 128], qtk_lo[:, ssl],
                        identb[:, :]), inc="st" if r == 3 else None)
                    wtB = (list(qk_ev[(1, gl, g // 2)]) + [bank_free[bB]]) if r == 0 else []
                    p.emit("t", lambda t, ssl=ssl, bB16=bB16, r=r: t.transpose(
                        bB16[0:64, 128 * r:128 * r + 128], ktk_hi[:, ssl], identb[:, :]),
                        waits=wtB)
                    ttrB = p.emit("t", lambda t, ssl=ssl, bB16=bB16, r=r: t.transpose(
                        bB16[64:128, 128 * r:128 * r + 128], ktk_lo[:, ssl], identb[:, :]),
                        inc="st" if r == 3 else None)
                eng, sm = ("v", "sv") if g % 2 == 0 else ("s", "ss")
                t1 = p.emit(eng, lambda e, g=g, bA16=bA16: ecopy(e,
                    bass.AP(qp_hi, 4 * g, [[2048, 128], [1, 4], [16, 128]]),
                    bA16[:, 0:512]), waits=[ttrA], inc=sm)
                t2 = p.emit(eng, lambda e, g=g, bA16=bA16: ecopy(e,
                    bass.AP(qp_lo, 4 * g, [[2048, 128], [1, 4], [16, 128]]),
                    bA16[:, 512:1024]), inc=sm)
                bank_free[bA] = t2
                t3 = p.emit(eng, lambda e, g=g, bB16=bB16: ecopy(e,
                    bass.AP(kp, 4 * g, [[2048, 128], [1, 4], [16, 128]]),
                    bB16[:, 0:512]), waits=[ttrB], inc=sm)
                bank_free[bB] = t3
            tok_qkg = p.tok("sv")
            tok_qkg_s = p.tok("ss")

            def emit_S(kt, qb_):
                bS = 4 + (kt % 4)
                p.emit("t", lambda t, kt=kt, qb_=qb_, bS=bS: t.matmul(
                    pb[bS][:, :], kp[:, 128 * kt:128 * kt + 128],
                    qp_hi[:, 512 * qb_:512 * qb_ + 512], start=True, stop=False),
                    waits=[tok_qkg, tok_qkg_s, bank_free[bS]])
                tS = p.emit("t", lambda t, kt=kt, qb_=qb_, bS=bS: t.matmul(
                    pb[bS][:, :], kp[:, 128 * kt:128 * kt + 128],
                    qp_lo[:, 512 * qb_:512 * qb_ + 512], start=False, stop=True),
                    inc="st")
                if kt >= 4 * qb_:
                    j = kt - 4 * qb_
                    tS = p.emit("v", lambda v, bS=bS, j=j: v.tensor_tensor(
                        pb[bS][:, :], pb[bS][:, :],
                        maskT[:, 512 * j:512 * j + 512], OP.add),
                        waits=[tS], inc="sv")
                pslot = kt % 4
                tE = p.emit("s", lambda s, bS=bS, pslot=pslot: s.activation(
                    pt[pslot], pb[bS][:, :], AF.Exp,
                    bias=cvec[:, C_EXPB:C_EXPB + 1], scale=0.125),
                    waits=[tS, pt_guard[pslot]], inc="ss")
                bank_free[bS] = tE
                return tE

            pend = [None]
            for qb_ in range(4):
                nk = 4 * qb_ + 4
                bO = 2 + (qb_ % 2)
                bR = qb_ % 2
                tEs = {}
                for kk_ in range(min(3, nk)):
                    tEs[kk_] = emit_S(kk_, qb_)
                tok_pv = None
                for kt in range(nk):
                    if kt == 2 and pend[0] is not None:
                        pend[0]()
                        pend[0] = None
                    if kt + 3 < nk:
                        tEs[kt + 3] = emit_S(kt + 3, qb_)
                    pslot = kt % 4
                    w_pv = [tEs[kt], tok_vg]
                    if kt == 0:
                        w_pv.append(bank_free[bO])
                    tok_pv = p.emit("t", lambda t, kt=kt, pslot=pslot, nk=nk, bO=bO:
                                    t.matmul(
                        pb[bO][0:65, :], vgo[:, 65 * kt:65 * kt + 65], pt[pslot],
                        start=(kt == 0), stop=(kt == nk - 1)),
                        waits=w_pv, inc="st")
                    pt_guard[pslot] = tok_pv
                tden = p.emit("v", lambda v, bO=bO: v.reciprocal(
                    rr2[64:65, 0:512], pb[bO][64:65, :]),
                    waits=[tok_pv, rr_guard[0]], inc="sv")

                def mk_pend(qb_, bO, bR, tden):
                    def flush():
                        trb = p.emit("t", lambda t, bR=bR: t.matmul(
                            pb[bR][:, :], onest[64:65, 0:128], rr2[64:65, 0:512],
                            start=True, stop=True),
                            waits=[tden, bank_free[bR]], inc="st")
                        rr_guard[0] = trb
                        trbe = p.emit("s", lambda s, bR=bR: s.activation(
                            rb[:, :], pb[bR][:, :], AF.Identity,
                            bias=cvec[:, C_ZERO:C_ZERO + 1], scale=1.0),
                            waits=[trb, rb_guard[0]], inc="ss")
                        bank_free[bR] = trbe
                        tot = p.emit("v", lambda v, qb_=qb_, bO=bO: v.tensor_tensor(
                            ot[:, 512 * qb_:512 * qb_ + 512], pb[bO][0:64, :],
                            rb[0:64, :], OP.mult),
                            waits=[trbe, ot_guard[0]], inc="sv")
                        rb_guard[0] = tot
                        bank_free[bO] = tot
                    return flush

                pend[0] = mk_pend(qb_, bO, bR, tden)
            pend[0]()
            pend[0] = None
            tok_ot = p.tok("sv")

            for blk in range(2):
                b = blk
                p.emit("t", lambda t, blk=blk, b=b: t.matmul(
                    pb[b][:, :], eperm[:, 0:128],
                    bass.AP(ot, 8 * blk, [[2048, 64], [2, 4], [16, 128]]),
                    start=True, stop=False),
                    waits=[tok_ot, bank_free[b]])
                tui = p.emit("t", lambda t, blk=blk, b=b: t.matmul(
                    pb[b][:, :], eperm[:, 128:256],
                    bass.AP(ot, 8 * blk + 1, [[2048, 64], [2, 4], [16, 128]]),
                    start=False, stop=True),
                    inc="st")
                ot_guard[0] = tui
                eng, sm = ("v", "sv") if blk == 0 else ("s", "ss")
                tev = p.emit(eng, lambda e, blk=blk, b=b, gl=gl: ecopy(e,
                    bass.AP(aot, 1024 * blk + 128 * gl,
                            [[2048, 128], [256, 4], [1, 128]]),
                    pb[b][:, :]),
                    waits=[tui], inc=sm)
                bank_free[b] = tev
            if gl == 0 and h < 7:
                xnT_toks = emit_ln1_post(h + 1, ln1_pre, banks=(2, 3),
                                         evac_waits=[tok_v_mms])
        tok_aot = p.tok("sv")
        tok_aot_s = p.tok("ss")

        kfw, tlf = load_w(fw2, h)
        tok_wqz, _ = quant(wb[kfw][:, :], svw[:, 0:1],
                           wqz[:, 1024 * h:1024 * h + 1024], [tlf, tok_swb])
        wb_free[kfw] = tok_wqz
        zn_guard[0] = tok_wqz if h == 7 else zn_guard[0]

        pr_done = []
        for c in range(8):
            kpw, tlp = load_w(pw2, c)
            for mt in range(2):
                for oh in range(2):
                    b = 2 * mt + oh
                    w_s = [tlp, tok_aot, tok_aot_s]
                    if c == 0:
                        w_s.append(bank_free[b])
                    last = (mt == 1 and oh == 1)
                    tm = p.emit("t", lambda t, c=c, mt=mt, oh=oh, b=b, kpw=kpw: t.matmul(
                        pb[b][:, :], aot[:, 256 * c + 128 * mt:256 * c + 128 * mt + 128],
                        wb[kpw][:, 512 * oh:512 * oh + 512],
                        start=(c == 0), stop=(c == 7)),
                        waits=w_s, inc="st" if last or c == 7 else None)
                    if c == 7:
                        pr_done.append((b, tm, mt, oh))
            wb_free[kpw] = p.tok("st")
        for mt in range(2):
            i = 2 * h + mt
            tz = []
            for (b, tmm, mt_, oh) in [x for x in pr_done if x[2] == mt]:
                tzz = p.emit("v", lambda v, b=b, oh=oh: v.scalar_tensor_tensor(
                    z0[:, 512 * oh:512 * oh + 512], pb[b][:, :], 1.0,
                    biasP[:, 512 * oh:512 * oh + 512], OP.mult, OP.add),
                    waits=[tmm, z0_guard[0]], inc="sv")
                bank_free[b] = tzz
                tz.append(tzz)
            wstat = layernorm_stats(z0[:, :], tz)
            tzn = p.emit("s", lambda s: s.activation(zn[:, :], z0[:, :], AF.Identity,
                                                     bias=nmr[:, 0:1], scale=rstd[:, 0:1]),
                         waits=[wstat, zn_guard[0]], inc="ss")
            ln_guard[0] = tzn
            z0_guard[0] = tzn
            q_guard[0] = tzn
            tadd = p.emit("v", lambda v, i=i: v.tensor_tensor(
                x_res[:, 1024 * i:1024 * (i + 1)], x_res[:, 1024 * i:1024 * (i + 1)],
                zn[:, :], OP.add),
                waits=[tzn], inc="sv")
            zn_guard[0] = tadd
            wstat = layernorm_stats(x_res[:, 1024 * i:1024 * (i + 1)], [tadd])
            p.emit("v", lambda v, i=i: v.tensor_copy(r3[:, i:i + 1], rstd[:, :]),
                   waits=[wstat], inc="sv")
            p.emit("v", lambda v, i=i: v.tensor_copy(nm3[:, i:i + 1], nmr[:, :]), inc="sv")
            p.emit("v", lambda v, i=i: v.tensor_reduce(
                mxs[:, i:i + 1], x_res[:, 1024 * i:1024 * (i + 1)], AX, OP.max), inc="sv")
            p.emit("v", lambda v, i=i: v.tensor_reduce(
                mns[:, i:i + 1], x_res[:, 1024 * i:1024 * (i + 1)], AX, OP.min), inc="sv")
            p.emit("v", lambda v, i=i: v.tensor_tensor(t_a[:, :], mxs[:, i:i + 1],
                                                       mv[:, 0:1], OP.subtract), inc="sv")
            p.emit("v", lambda v, i=i: v.tensor_tensor(t_b[:, :], mv[:, 0:1],
                                                       mns[:, i:i + 1], OP.subtract),
                   inc="sv")
            p.emit("v", lambda v: v.tensor_tensor(t_a[:, :], t_a[:, :], t_b[:, :], OP.max),
                   inc="sv")
            tra = p.emit("v", lambda v, i=i: v.tensor_tensor(ramax[:, i:i + 1], t_a[:, :],
                                                             rstd[:, :], OP.mult), inc="sv")
    tok_x2 = p.tok("sv")
    dump("d_x2", x_res[:, :], [tok_x2])
    dump("d_wqz", wqz[:, :], [tok_wqz])

    tam = p.emit("v", lambda v: v.tensor_reduce(wred[:, :], ramax[:, :], AX, OP.max),
                 waits=[tra], inc="sv")
    tam = p.emit("t", lambda t: t.transpose(pb[6][0:1, 0:128], wred[:, :], ident[:, :]),
                 waits=[tam, bank_free[6]], inc="st")
    tam = p.emit("v", lambda v: v.tensor_reduce(sc1[0:1, 0:1], pb[6][0:1, 0:128], AX,
                                                OP.max),
                 waits=[tam], inc="sv")
    bank_free[6] = tam
    tcc = p.emit("g", lambda g: g.dma_start(out=ccin.ap(), in_=sc1[0:1, 0:1]),
                 waits=[tam], inc="sg16", amt=16)
    tcc = p.emit("g", lambda g: g.collective_compute(
        "AllReduce", OP.max, replica_groups=[list(range(8))],
        ins=[ccin.ap().opt()], outs=[ccout.ap().opt()]),
        waits=[tcc], inc="scc")
    tcc = p.emit("g", lambda g: g.dma_start(out=sc1[0:1, 0:1], in_=ccout.ap()),
                 waits=[tcc], inc="sg16", amt=16)
    tsx = p.emit("v", lambda v: v.tensor_scalar(sc1[0:1, 0:1], sc1[0:1, 0:1], 1e-12, None,
                                                OP.max), waits=[tcc], inc="sv")
    tsx = p.emit("v", lambda v: v.reciprocal(sc1[0:1, 1:2], sc1[0:1, 0:1]),
                 waits=[tsx], inc="sv")
    tsx = p.emit("v", lambda v: v.tensor_scalar(sc1[0:1, 0:1], sc1[0:1, 1:2], 448.0, 448.0,
                                                OP.mult, OP.min), waits=[tsx], inc="sv")
    tix = p.emit("v", lambda v: v.reciprocal(sc1[0:1, 1:2], sc1[0:1, 0:1]),
                 waits=[tsx], inc="sv")
    tix = p.emit("v", lambda v: v.tensor_tensor(sc1[0:1, 1:2], sc1[0:1, 1:2], sc1[0:1, 3:4],
                                                OP.mult), waits=[tix], inc="sv")
    tbx = p.emit("t", lambda t: t.matmul(pb[6][:, 0:1], onest[0:1, 0:128], sc1[0:1, 0:1],
                                         start=True, stop=True),
                 waits=[tsx, bank_free[6]], inc="st")
    tbx2 = p.emit("v", lambda v: v.tensor_copy(svx[:, :], pb[6][:, 0:1]),
                  waits=[tbx], inc="sv")
    bank_free[6] = tbx2
    tbi = p.emit("t", lambda t: t.matmul(pb[7][:, 0:1], onest[0:1, 0:128], sc1[0:1, 1:2],
                                         start=True, stop=True),
                 waits=[tix, bank_free[7]], inc="st")
    tbi2 = p.emit("v", lambda v: v.tensor_copy(ixw[:, :], pb[7][:, 0:1]),
                  waits=[tbi], inc="sv")
    bank_free[7] = tbi2

    trlast_par = {0: None, 1: None}
    tq_par = {0: None, 1: None}
    o2sl = [ptA[:, 1024:1536].bitcast(BF16), ptA[:, 1536:2048].bitcast(BF16)]
    ffn_scr = [pt_guard[2], pt_guard[3], xnT_toks[0], xnT_toks[1]]
    o2_free = {0: list(ffn_scr), 1: list(ffn_scr)}
    lastF = [None]
    fdh = [None]
    xq_ev = {0: None, 1: None}

    def back_head_a(i):
        if i < 0 or i >= NT:
            return
        tq = tq_par[i % 2]
        tr_last = None
        for g in range(2):
            bk = 4 + g
            for r in range(4):
                c = 4 * g + r
                tr_last = p.emit("t", lambda t, c=c, bk=bk, r=r, i=i: t.transpose(
                    pb[bk][:, :].bitcast(BF16)[:, 128 * r:128 * r + 128],
                    xq[i % 2][:, 128 * c:128 * c + 128], identb[:, :]),
                    waits=[tq, bank_free[bk]] if r == 0 else [],
                    inc="st" if r == 3 else None)
            eng, sm = ("v", "sv") if g == 0 else ("s", "ss")
            tev = p.emit(eng, lambda e, bk=bk, g=g: ecopy(e,
                xqT[:, 512 * g:512 * g + 512],
                pb[bk][:, :].bitcast(BF16)[:, 0:512]),
                waits=[tr_last, lastF[0]], inc=sm)
            bank_free[bk] = tev
            xq_ev[g] = tev
        trlast_par[i % 2] = tr_last

    def back_head_b(i):
        if i < 0 or i >= NT:
            return
        tok_xqT = xq_ev[0]
        tok_xqT_s = xq_ev[1]
        f_done = []
        for oh in range(2):
            bk = oh
            for c in range(8):
                w_s = [tok_xqT, tok_xqT_s, tok_wqz]
                if c == 0:
                    w_s.append(bank_free[bk])
                tf = p.emit("t", lambda t, c=c, oh=oh, bk=bk: t.matmul(
                    pb[bk][:, :], xqT[:, 128 * c:128 * c + 128],
                    wqz[:, 1024 * c + 512 * oh:1024 * c + 512 * oh + 512],
                    start=(c == 0), stop=(c == 7)),
                    waits=w_s, inc="st" if c == 7 else None)
                if c == 7:
                    f_done.append((bk, tf, oh))
        lastF[0] = f_done[-1][1]
        fdh[0] = f_done

    def back_tail(j):
        if j < 0:
            return
        o2c = o2sl[j % 2]
        tz = []
        for (bk, tmm, oh) in fdh[0]:
            tzz = p.emit("v", lambda v, bk=bk, oh=oh, o2c=o2c: v.scalar_tensor_tensor(
                o2c[:, 512 * oh:512 * oh + 512], pb[bk][:, :], ixw[:, 0:1],
                biasF[:, 512 * oh:512 * oh + 512], OP.mult, OP.add),
                waits=[tmm] + list(o2_free[j % 2]), inc="sv")
            bank_free[bk] = tzz
            tz.append(tzz)
        wstat = layernorm_stats(o2c[:, :], tz)
        tzn = p.emit("s", lambda s, o2c=o2c: s.activation(
            zn[:, :], o2c[:, :], AF.Identity,
            bias=nmr[:, 0:1], scale=rstd[:, 0:1]),
            waits=[wstat, zn_guard[0]], inc="ss")
        ln_guard[0] = tzn
        o2_free[j % 2] = [tzn]
        tfin = p.emit("v", lambda v, j=j: v.tensor_tensor(
            x_res[:, 1024 * j:1024 * (j + 1)], x_res[:, 1024 * j:1024 * (j + 1)],
            zn[:, :], OP.add),
            waits=[tzn], inc="sv")
        zn_guard[0] = tfin
        p.emit("y", lambda y, j=j: y.dma_start(
            out=bass.AP(out, 128 * j * 1024, [[1024, 128], [1, 1024]]),
            in_=x_res[:, 1024 * j:1024 * (j + 1)]),
            waits=[tfin], inc="sout", amt=16)

    for ii in range(NT + 2):
        def mid(ii=ii):
            back_head_a(ii - 1)
            back_tail(ii - 2)
            back_head_b(ii - 1)
        if ii < NT:
            f = ii
            txn2 = p.emit("s", lambda s, f=f: s.activation(
                z0[:, :], x_res[:, 1024 * f:1024 * (f + 1)], AF.Identity,
                bias=nm3[:, f:f + 1], scale=r3[:, f:f + 1]),
                waits=[tbx2, tbi2, z0_guard[0], dgd("d_x2")], inc="ss")
            xq_guard[0] = trlast_par[f % 2]
            tq_par[f % 2], tq1 = quant(z0[:, :], svx[:, 0:1], xq[f % 2][:, :],
                                       [txn2], vbank=6, mid=mid)
            z0_guard[0] = tq1
        else:
            mid()

    p.emit("g", lambda g: g.nop(), waits=[("sout", 16 * NT)])

    with es, nc.Block() as block:
        p.bind(block)
    return nc


def make_consts():
    ident = np.eye(128, dtype=np.float32)
    identb = np.eye(128, dtype=np.float32).astype(ml_dtypes.bfloat16)
    onest = np.ones((128, 128), np.float32)
    eperm = np.zeros((64, 256), np.float32)
    for k in range(64):
        eperm[k, k] = 1.0
        eperm[k, 128 + 64 + k] = 1.0
    maskT = np.zeros((128, 2048), np.float32)
    tp = np.arange(128)
    tc = np.arange(512)
    for j in range(4):
        mpl = 8 * j + tp // 16
        ip = tp % 16
        ml = tc // 16
        iq = tc % 16
        m = (mpl[:, None] > ml[None, :]) | ((mpl[:, None] == ml[None, :]) &
                                            (ip[:, None] > iq[None, :]))
        maskT[:, 512 * j:512 * (j + 1)] = np.where(m, -30000.0, 0.0)
    cvec = np.zeros((128, 8), np.float32)
    cvec[:, C_EXPB] = -12.0
    cvec[:, C_EPS] = EPS
    cvec[:, C_MAG] = 24576.0
    cvec[:, C_NMAG] = -24576.0
    return ident, identb, onest, eperm, maskT.astype(ml_dtypes.float8_e5m2), cvec


_NC_CACHE = [None]


def kernel(L0_attn_x, L0_attn_Wq, L0_attn_Wk, L0_attn_Wv, L0_attn_proj_w,
           L0_attn_proj_b, L0_ffn_weight, L0_ffn_bias):
    x = np.ascontiguousarray(np.asarray(L0_attn_x, dtype=np.float32))
    Wq = np.asarray(L0_attn_Wq, np.float32)
    Wk = np.asarray(L0_attn_Wk, np.float32)
    Wv = np.asarray(L0_attn_Wv, np.float32)
    pw = np.asarray(L0_attn_proj_w, np.float32)
    pb_ = np.asarray(L0_attn_proj_b, np.float32)
    fw = np.asarray(L0_ffn_weight, np.float32)
    fb = np.asarray(L0_ffn_bias, np.float32)

    wq2 = np.ascontiguousarray(Wq.transpose(1, 0, 2).reshape(1024, 1024))
    wk2 = np.ascontiguousarray(Wk.transpose(1, 0, 2).reshape(1024, 1024))
    wv2 = np.ascontiguousarray(Wv.transpose(1, 0, 2).reshape(1024, 1024))
    pw2 = np.ascontiguousarray(pw.T)
    fw2 = np.ascontiguousarray(fw.T)
    biasP = np.ascontiguousarray(np.broadcast_to(pb_, (128, 1024)))
    biasF = np.ascontiguousarray(np.broadcast_to(fb, (128, 1024)))
    ident, identb, onest, eperm, maskT, cvec = make_consts()

    if _NC_CACHE[0] is None:
        _NC_CACHE[0] = build_nc()
    nc = _NC_CACHE[0]

    in_maps = []
    for c in range(8):
        in_maps.append(dict(
            x_in=np.ascontiguousarray(x[c]), wq2=wq2, wk2=wk2, wv2=wv2, pw2=pw2,
            fw2=fw2, biasP=biasP, biasF=biasF, maskT=maskT, eperm=eperm,
            ident=ident, identb=identb, onest=onest, cvec=cvec))
    res = run_bass_kernel_spmd(nc, in_maps, core_ids=list(range(8)))
    return np.stack([r["out"] for r in res.results], axis=0)


# version 22
# speedup vs baseline: 1.0430x; 1.0430x over previous
import os
import numpy as np
import ml_dtypes
import concourse.bass as bass
import concourse.mybir as mybir
from concourse.bass_utils import run_bass_kernel_spmd
from concourse.alu_op_type import AluOpType

DBG = os.environ.get("KDBG") == "1"
F32 = mybir.dt.float32
BF16 = mybir.dt.bfloat16
AF = mybir.ActivationFunctionType
AX = mybir.AxisListType.X
OP = AluOpType

NT = 16
EPS = 1e-5
C_EXPB, C_EPS, C_MAG, C_NMAG, C_ZERO = 0, 1, 2, 3, 6


class P:

    def __init__(self, nc, es):
        self.nc = nc
        self.es = es
        self.ops = {e: [] for e in ("t", "v", "s", "g", "y")}
        self.sems = {}
        self.cnt = {}

    def sem(self, name):
        self.sems[name] = self.es.enter_context(self.nc.semaphore(name))
        self.cnt[name] = 0

    def emit(self, eng, fn, waits=(), inc=None, amt=1):
        selfwait = None
        if eng == "v":
            selfwait = ("sv", self.cnt["sv"])
            if inc is None:
                inc = "sv"
        elif eng == "s":
            selfwait = ("ss", self.cnt["ss"])
            if inc is None:
                inc = "ss"
        tok = None
        if inc is not None:
            self.cnt[inc] += amt
            tok = (inc, self.cnt[inc])
        wl = [w for w in waits if w is not None]
        if selfwait is not None and selfwait[1] > 0:
            wl.append(selfwait)
        self.ops[eng].append((wl, fn, (inc, amt) if inc else None))
        return tok

    def tok(self, sem):
        return (sem, self.cnt[sem])

    def bind(self, block):
        sems = self.sems

        def runner(ops):
            def _(eng):
                hiw = {}
                for waits, fn, inc in ops:
                    for (s, v) in waits:
                        if hiw.get(s, -1) < v:
                            eng.wait_ge(sems[s], v)
                            hiw[s] = v
                    ins = fn(eng)
                    if inc is not None:
                        if ins is None:
                            eng.sem_inc(sems[inc[0]], inc[1])
                        else:
                            ins.then_inc(sems[inc[0]], inc[1])
            return _

        block.tensor(runner(self.ops["t"]))
        block.vector(runner(self.ops["v"]))
        block.scalar(runner(self.ops["s"]))
        block.gpsimd(runner(self.ops["g"]))
        block.sync(runner(self.ops["y"]))


def build_nc():
    from contextlib import ExitStack
    nc = bass.Bass("TRN2", target_bir_lowering=False, detect_race_conditions=False)

    x_in = nc.dram_tensor("x_in", [2048, 1024], F32, kind="ExternalInput")
    wq2 = nc.dram_tensor("wq2", [1024, 1024], F32, kind="ExternalInput")
    wk2 = nc.dram_tensor("wk2", [1024, 1024], F32, kind="ExternalInput")
    wv2 = nc.dram_tensor("wv2", [1024, 1024], F32, kind="ExternalInput")
    pw2 = nc.dram_tensor("pw2", [1024, 1024], F32, kind="ExternalInput")
    fw2 = nc.dram_tensor("fw2", [1024, 1024], F32, kind="ExternalInput")
    biasP_in = nc.dram_tensor("biasP", [128, 1024], F32, kind="ExternalInput")
    biasF_in = nc.dram_tensor("biasF", [128, 1024], F32, kind="ExternalInput")
    maskT_in = nc.dram_tensor("maskT", [128, 2048], mybir.dt.float8e5, kind="ExternalInput")
    eperm_in = nc.dram_tensor("eperm", [64, 256], F32, kind="ExternalInput")
    ident_in = nc.dram_tensor("ident", [128, 128], F32, kind="ExternalInput")
    identb_in = nc.dram_tensor("identb", [128, 128], BF16, kind="ExternalInput")
    onest_in = nc.dram_tensor("onest", [128, 128], F32, kind="ExternalInput")
    cvec_in = nc.dram_tensor("cvec", [128, 8], F32, kind="ExternalInput")
    out = nc.dram_tensor("out", [2048, 1024], F32, kind="ExternalOutput")
    dbg = {}
    if DBG:
        for nm, shp in (("d_xnT", [128, 2048]), ("d_x2", [128, 16384]),
                        ("d_wqz", [128, 8192])):
            dbg[nm] = nc.dram_tensor(nm, shp, BF16 if nm == "d_wqz" else F32,
                                     kind="ExternalOutput")
    ccin = nc.dram_tensor("ccin", [1, 1], F32)
    ccout = nc.dram_tensor("ccout", [1, 1], F32)

    es = ExitStack()
    S = lambda *a: es.enter_context(nc.sbuf_tensor(*a))

    x_res = S("x_res", [128, NT * 1024], F32)
    xnT = S("xnT", [128, 8 * 256], F32)
    qtk_hi = S("qtk_hi", [128, 2 * 1024], BF16)
    qtk_lo = S("qtk_lo", [128, 2 * 1024], BF16)
    ktk_hi = S("ktk_hi", [128, 2 * 1024], BF16)
    ktk_lo = S("ktk_lo", [128, 2 * 1024], BF16)
    vtok = S("vtok", [128, 2 * 1024], F32)
    qp_hi = S("qp_hi", [128, 2048], BF16)
    qp_lo = S("qp_lo", [128, 2048], BF16)
    kp = S("kp", [128, 2048], BF16)
    rr2 = S("rr2", [65, 512], F32)
    vgo = S("vgo", [128, 16 * 65], F32)
    ptA = S("ptA", [128, 4 * 512], F32)
    pt = [ptA[:, 512 * k:512 * k + 512] for k in range(4)]
    xn_sc = ptA[:, 1024:2048]
    ot = S("ot", [64, 2048], F32)
    rb = S("rb", [128, 512], F32)
    aot = S("aot", [128, 8 * 256], F32)
    z0 = S("z0", [128, 1024], F32)
    wb = [S(f"wb{k}", [128, 1024], F32) for k in range(2)]
    wqz = S("wqz", [128, 8 * 1024], BF16)
    xq = [S(f"xq{k}", [128, 1024], BF16) for k in range(2)]
    xqT = S("xqT", [128, 1024], BF16)
    qa = S("qa", [128, 1024], F32)
    zn = S("zn", [128, 1024], F32)
    qc = S("qc", [128, 1024], F32)
    qd = ptA[:, 0:512].bitcast(BF16)
    qm = S("qm", [128, 1024], mybir.dt.uint8)
    biasP = S("biasP_s", [128, 1024], F32)
    biasF = S("biasF_s", [128, 1024], F32)
    maskT = S("maskT_s", [128, 2048], mybir.dt.float8e5)
    eperm = S("eperm_s", [64, 256], F32)
    ident = S("ident_s", [128, 128], F32)
    identb = S("identb_s", [128, 128], BF16)
    onest = S("onest_s", [128, 128], F32)
    cvec = S("cvec_s", [128, 8], F32)
    bnst = S("bnst", [128, 12], F32)
    mv = S("mv", [128, 2], F32)
    t_a = S("t_a", [128, 1], F32)
    t_b = S("t_b", [128, 1], F32)
    rstd = S("rstd", [128, 1], F32)
    nmr = S("nmr", [128, 1], F32)
    r3 = S("r3", [128, NT], F32)
    nm3 = S("nm3", [128, NT], F32)
    mxs = S("mxs", [128, NT], F32)
    mns = S("mns", [128, NT], F32)
    ramax = S("ramax", [128, NT], F32)
    wA = S("wA", [128, 8], F32)
    wred = S("wred", [128, 1], F32)
    sc1 = S("sc1", [128, 4], F32)
    svx = S("svx", [128, 1], F32)
    svw = S("svw", [128, 1], F32)
    ixw = S("ixw", [128, 1], F32)

    pb = [es.enter_context(nc.psum_tensor(f"pb{k}", [128, 512], F32)) for k in range(8)]

    p = P(nc, es)
    for s in ("sy", "sw0", "sw1", "st", "sv", "ss", "sg16", "sgc", "scc", "sout"):
        p.sem(s)

    SW = ("sw0", "sw1")
    wb_free = [None, None]
    wslot = [0]

    def load_w(dram, c):
        k = wslot[0] % 2
        wslot[0] += 1
        src = bass.AP(dram, 128 * c * 1024, [[1024, 128], [1, 1024]])
        tok = p.emit("y", lambda y, s=src, k=k: y.dma_start(out=wb[k][:, :], in_=s),
                     waits=[wb_free[k]], inc=SW[k], amt=16)
        return k, tok

    def ecopy(e, dst, srcv):
        return e.tensor_copy(dst, srcv) if hasattr(e, "tensor_copy") else e.copy(dst, srcv)

    bank_free = [None] * 8
    ln_guard = [None]
    q_guard = [None]
    xq_guard = [None]
    pt_guard = {0: None, 1: None, 2: None, 3: None}
    ot_guard = [None]
    rb_guard = [None]
    rr_guard = [None]
    vg_guard = [None]
    zn_guard = [None]
    z0_guard = [None]

    dbw = {}

    def dump(nm, src_ap, waits):
        if DBG:
            dbw[nm] = p.emit("y", lambda y, nm=nm, s=src_ap: y.dma_start(
                out=dbg[nm].ap(), in_=s), waits=waits, inc="sout", amt=16)

    def dgd(nm):
        return dbw.get(nm)

    p.emit("y", lambda y: y.dma_start(
        out=bass.AP(x_res, 0, [[NT * 1024, 128], [1024, NT], [1, 1024]]),
        in_=bass.AP(x_in, 0, [[1024, 128], [131072, NT], [1, 1024]])), inc="sy", amt=16)
    for dst, src in ((biasP, biasP_in), (biasF, biasF_in), (maskT, maskT_in),
                     (eperm, eperm_in), (ident, ident_in), (identb, identb_in),
                     (onest, onest_in), (cvec, cvec_in)):
        p.emit("y", lambda y, d=dst, s=src: y.dma_start(out=d[:, :], in_=s.ap()),
               inc="sy", amt=16)
    tok_consts = p.tok("sy")
    tok_x = tok_consts

    tok_vones = p.emit("g", lambda g: g.memset(
        bass.AP(vgo, 64, [[1040, 128], [65, 16]]), 1.0), inc="sgc")

    for c in range(8):
        k, tokl = load_w(fw2, c)
        tokr = p.emit("v", lambda v, k=k, c=c: v.tensor_reduce(
            wA[:, c:c + 1], wb[k][:, :], AX, OP.max, apply_absolute_value=True),
            waits=[tokl], inc="sv")
        wb_free[k] = tokr
    tok = p.emit("v", lambda v: v.tensor_reduce(wred[:, :], wA[:, :], AX, OP.max), inc="sv")
    tok = p.emit("t", lambda t: t.transpose(pb[6][0:1, 0:128], wred[:, :], ident[:, :]),
                 waits=[tok, tok_consts, bank_free[6]], inc="st")
    tok = p.emit("v", lambda v: v.tensor_reduce(sc1[0:1, 2:3], pb[6][0:1, 0:128], AX, OP.max),
                 waits=[tok], inc="sv")
    bank_free[6] = tok
    tok = p.emit("v", lambda v: v.tensor_scalar(sc1[0:1, 2:3], sc1[0:1, 2:3], 1e-12, None,
                                                OP.max), waits=[tok], inc="sv")
    tok = p.emit("v", lambda v: v.reciprocal(sc1[0:1, 3:4], sc1[0:1, 2:3]),
                 waits=[tok], inc="sv")
    tok = p.emit("v", lambda v: v.tensor_scalar(sc1[0:1, 2:3], sc1[0:1, 3:4], 448.0, 448.0,
                                                OP.mult, OP.min), waits=[tok], inc="sv")
    tok_sw = p.emit("v", lambda v: v.reciprocal(sc1[0:1, 3:4], sc1[0:1, 2:3]),
                    waits=[tok], inc="sv")
    tok = p.emit("t", lambda t: t.matmul(pb[6][:, 0:1], onest[0:1, 0:128], sc1[0:1, 2:3],
                                         start=True, stop=True),
                 waits=[tok_sw, bank_free[6]], inc="st")
    tok_swb = p.emit("v", lambda v: v.tensor_copy(svw[:, :], pb[6][:, 0:1]),
                     waits=[tok], inc="sv")
    bank_free[6] = tok_swb

    def quant(src_ap, svec_col, dst_ap, waits, vbank=6, mid=None):
        t1 = p.emit("v", lambda v: v.tensor_scalar(qa[:, :], src_ap, svec_col, 448.0,
                                                   OP.mult, OP.min),
                    waits=list(waits) + [q_guard[0]], inc="sv")
        t2 = p.emit("v", lambda v: v.tensor_scalar(qa[:, :], qa[:, :], -448.0, None, OP.max),
                    waits=[t1], inc="sv")
        if mid is not None:
            mid()
        if True:
            vb2 = [pb[vbank][:, :], pb[vbank + 1][:, :]]
            t3a = p.emit("s", lambda s: s.activation(vb2[0], qa[:, 0:512], AF.Identity,
                                                     bias=cvec[:, C_ZERO:C_ZERO + 1],
                                                     scale=1048577.0),
                         waits=[t2, bank_free[vbank]], inc="ss")
            t3 = p.emit("s", lambda s: s.activation(vb2[1], qa[:, 512:1024], AF.Identity,
                                                    bias=cvec[:, C_ZERO:C_ZERO + 1],
                                                    scale=1048577.0),
                        waits=[bank_free[vbank + 1]], inc="ss")
            t4 = None
            for hf in range(2):
                t4 = p.emit("v", lambda v, hf=hf: v.tensor_tensor(
                    qc[:, 512 * hf:512 * hf + 512], vb2[hf], qa[:, 512 * hf:512 * hf + 512],
                    OP.subtract), waits=[t3a, t3], inc="sv")
            t5 = None
            for hf in range(2):
                t5 = p.emit("v", lambda v, hf=hf, dst_ap=dst_ap: v.tensor_tensor(
                    dst_ap[:, 512 * hf:512 * hf + 512], vb2[hf],
                    qc[:, 512 * hf:512 * hf + 512], OP.subtract),
                    waits=[t4, xq_guard[0]], inc="sv")
            bank_free[vbank] = t5
            bank_free[vbank + 1] = t5
        t6 = p.emit("s", lambda s: s.activation(qc[:, :], qa[:, :], AF.Identity,
                                                bias=cvec[:, C_MAG:C_MAG + 1], scale=1.0),
                    waits=[t2, t5], inc="ss")
        t7 = p.emit("s", lambda s: s.activation(qc[:, :], qc[:, :], AF.Identity,
                                                bias=cvec[:, C_NMAG:C_NMAG + 1], scale=1.0),
                    waits=[t6], inc="ss")
        t8 = p.emit("s", lambda s: s.activation(qd[:, :], qa[:, :], AF.Abs,
                                                bias=cvec[:, C_ZERO:C_ZERO + 1], scale=1.0),
                    waits=[t2, pt_guard[0]], inc="ss")
        t9 = p.emit("v", lambda v: v.tensor_scalar(qm[:, :], qd[:, :], 0.015625, None,
                                                   OP.is_lt),
                    waits=[t8], inc="sv")
        pt_guard[0] = t9
        t10 = p.emit("v", lambda v, dst_ap=dst_ap: v.copy_predicated(
            dst_ap, qm[:, :], qc[:, :]), waits=[t5, t7, t9], inc="sv")
        q_guard[0] = t8
        return t10, t1

    def layernorm_stats(src_ap, waits):
        w1 = p.emit("v", lambda v: v.bn_stats(bnst[:, 0:6], src_ap[:, 0:512]),
                    waits=list(waits), inc="sv")
        p.emit("v", lambda v: v.bn_stats(bnst[:, 6:12], src_ap[:, 512:1024]), inc="sv")
        w3 = p.emit("v", lambda v: v.bn_aggr(mv[:, :], bnst[:, :]),
                    waits=[ln_guard[0]], inc="sv")
        w4 = p.emit("s", lambda s: s.activation(t_a[:, :], mv[:, 1:2], AF.Sqrt,
                                                bias=cvec[:, C_EPS:C_EPS + 1], scale=1.0),
                    waits=[w3, tok_consts], inc="ss")
        ln_guard[0] = w4
        w5 = p.emit("v", lambda v: v.reciprocal(rstd[:, :], t_a[:, :]), waits=[w4], inc="sv")
        p.emit("v", lambda v: v.tensor_scalar(t_b[:, :], mv[:, 1:2], EPS, None, OP.add),
               inc="sv")
        p.emit("v", lambda v: v.tensor_tensor(t_a[:, :], rstd[:, :], rstd[:, :], OP.mult),
               inc="sv")
        p.emit("v", lambda v: v.tensor_tensor(t_a[:, :], t_a[:, :], t_b[:, :], OP.mult),
               inc="sv")
        p.emit("v", lambda v: v.tensor_scalar(t_a[:, :], t_a[:, :], -0.5, 1.5,
                                              OP.mult, OP.add), inc="sv")
        p.emit("v", lambda v: v.tensor_tensor(rstd[:, :], rstd[:, :], t_a[:, :], OP.mult),
               inc="sv")
        p.emit("v", lambda v: v.tensor_tensor(nmr[:, :], mv[:, 0:1], rstd[:, :], OP.mult),
               inc="sv")
        w12 = p.emit("v", lambda v: v.tensor_scalar(nmr[:, :], nmr[:, :], -1.0, None,
                                                    OP.mult), inc="sv")
        return w12

    def emit_ln1_pre(h):
        i0 = 2 * h
        w0 = layernorm_stats(x_res[:, 1024 * i0:1024 * (i0 + 1)], [tok_x])
        txn0 = p.emit("s", lambda s, i0=i0: s.activation(
            z0[:, :], x_res[:, 1024 * i0:1024 * (i0 + 1)], AF.Identity,
            bias=nmr[:, 0:1], scale=rstd[:, 0:1]),
            waits=[w0, z0_guard[0]], inc="ss")
        ln_guard[0] = txn0
        w1 = layernorm_stats(x_res[:, 1024 * (i0 + 1):1024 * (i0 + 2)], [tok_x])
        txn1 = p.emit("s", lambda s, i0=i0: s.activation(
            zn[:, :], x_res[:, 1024 * (i0 + 1):1024 * (i0 + 2)], AF.Identity,
            bias=nmr[:, 0:1], scale=rstd[:, 0:1]),
            waits=[w1, zn_guard[0]], inc="ss")
        ln_guard[0] = txn1
        return txn0, txn1

    def emit_ln1_post(h, pre, banks=(4, 5), evac_waits=()):
        txn0, txn1 = pre
        for il in range(2):
            txn = txn0 if il == 0 else txn1
            srcb = z0 if il == 0 else zn
            for g in range(2):
                b = banks[g]
                ttr = None
                for r in range(4):
                    c = 4 * g + r
                    ttr = p.emit("t", lambda t, c=c, b=b, r=r, srcb=srcb: t.transpose(
                        pb[b][:, 128 * r:128 * r + 128],
                        srcb[:, 128 * c:128 * c + 128], ident[:, :]),
                        waits=[txn, bank_free[b]] if r == 0 else [],
                        inc="st" if r == 3 else None)
                eng, sm = ("v", "sv") if g == 0 else ("s", "ss")
                tev = p.emit(eng, lambda e, b=b, g=g, il=il: ecopy(e,
                    bass.AP(xnT, 1024 * g + 128 * il, [[2048, 128], [256, 4], [1, 128]]),
                    pb[b][:, :]),
                    waits=[ttr, dgd("d_xnT")] + list(evac_waits), inc=sm)
                bank_free[b] = tev
                if il == 0:
                    z0_guard[0] = ttr
                else:
                    zn_guard[0] = ttr
        return p.tok("sv"), p.tok("ss")

    xnT_toks = emit_ln1_post(0, emit_ln1_pre(0))

    for h in range(8):
        tok_xnT, tok_xnT_s = xnT_toks

        qk_done = []
        for c in range(8):
            kq, tlq = load_w(wq2, c)
            kk, tlk = load_w(wk2, c)
            for mat in range(2):
                kslot = kq if mat == 0 else kk
                tl = tlq if mat == 0 else tlk
                for mt in range(2):
                    for oh in range(2):
                        b = 4 * mat + 2 * mt + oh
                        w_s = [tl, tok_xnT, tok_xnT_s]
                        if c == 0:
                            w_s.append(bank_free[b])
                        last = (mt == 1 and oh == 1)
                        tm = p.emit("t", lambda t, c=c, mt=mt, oh=oh, b=b, ks=kslot:
                                    t.matmul(
                            pb[b][:, :],
                            xnT[:, 256 * c + 128 * mt:256 * c + 128 * mt + 128],
                            wb[ks][:, 512 * oh:512 * oh + 512],
                            start=(c == 0), stop=(c == 7)),
                            waits=w_s, inc="st" if last or c == 7 else None)
                        if c == 7:
                            qk_done.append((b, tm, mat, mt, oh))
                wb_free[kslot] = p.tok("st")
        qk_ev = {}
        for (b, tmm, mat, mt, oh) in qk_done:
            dh = qtk_hi if mat == 0 else ktk_hi
            dl = qtk_lo if mat == 0 else ktk_lo
            sl = slice(1024 * mt + 512 * oh, 1024 * mt + 512 * oh + 512)
            eng, sm = ("v", "sv") if b % 2 == 0 else ("s", "ss")
            th = p.emit(eng, lambda e, b=b, dh=dh, sl=sl: ecopy(e, dh[:, sl], pb[b][:, :]),
                        waits=[tmm], inc=sm)
            tev = p.emit("v", lambda v, b=b, dh=dh, dl=dl, sl=sl: v.tensor_tensor(
                dl[:, sl], pb[b][:, :], dh[:, sl], OP.subtract),
                waits=[th], inc="sv")
            bank_free[b] = tev
            qk_ev[(mat, mt, oh)] = (th, tev)
        tok_qk_ev = p.tok("sv")
        tok_qk_es = p.tok("ss")

        v_done = []
        for c in range(8):
            kv, tlv = load_w(wv2, c)
            for mt in range(2):
                for oh in range(2):
                    b = 2 * mt + oh
                    w_s = [tlv]
                    if c == 0:
                        w_s.append(bank_free[b])
                    last = (mt == 1 and oh == 1)
                    tm = p.emit("t", lambda t, c=c, mt=mt, oh=oh, b=b, kv=kv: t.matmul(
                        pb[b][:, :],
                        xnT[:, 256 * c + 128 * mt:256 * c + 128 * mt + 128],
                        wb[kv][:, 512 * oh:512 * oh + 512],
                        start=(c == 0), stop=(c == 7)),
                        waits=w_s, inc="st" if last or c == 7 else None)
                    if c == 7:
                        v_done.append((b, tm, mt, oh))
            wb_free[kv] = p.tok("st")
        tok_v_mms = p.tok("st")
        for (b, tmm, mt, oh) in v_done:
            eng, sm = ("v", "sv") if b % 2 == 0 else ("s", "ss")
            tev = p.emit(eng, lambda e, b=b, mt=mt, oh=oh: ecopy(e,
                vtok[:, 1024 * mt + 512 * oh:1024 * mt + 512 * oh + 512], pb[b][:, :]),
                waits=[tmm, vg_guard[0]], inc=sm)
            bank_free[b] = tev
        tok_v_ev = p.tok("sv")
        tok_v_es = p.tok("ss")

        for gl in range(2):
            gsp = gl
            gw = [tok_v_ev, tok_v_es, tok_vones, bank_free[2], bank_free[3]]
            for j in range(16):
                vg_guard[0] = p.emit("g", lambda g, j=j, gl=gl: g.dma_start(
                    out=bass.AP(vgo, 65 * j, [[1040, 128], [1, 64]]),
                    in_=bass.AP(vtok, (8 * j) * 2048 + 1024 * gl,
                                [[2048, 8], [1, 1024]])),
                    waits=gw if j == 0 else [], inc="sg16", amt=16)
            tok_vg = vg_guard[0]
            for g in range(4):
                bA = 4 + 2 * (g % 2)
                bB = bA + 1
                bA16 = pb[bA][:, :].bitcast(BF16)
                bB16 = pb[bB][:, :].bitcast(BF16)
                ttrA = ttrB = None
                for r in range(4):
                    i = 4 * g + r
                    ssl = slice(1024 * gl + 64 * i, 1024 * gl + 64 * i + 64)
                    wtA = (list(qk_ev[(0, gl, g // 2)]) + [bank_free[bA]]) if r == 0 else []
                    p.emit("t", lambda t, ssl=ssl, bA16=bA16, r=r: t.transpose(
                        bA16[0:64, 128 * r:128 * r + 128], qtk_hi[:, ssl], identb[:, :]),
                        waits=wtA)
                    p.emit("t", lambda t, ssl=ssl, bA16=bA16, r=r: t.transpose(
                        bA16[64:128, 128 * r:128 * r + 128], qtk_hi[:, ssl], identb[:, :]))
                    p.emit("t", lambda t, ssl=ssl, bA16=bA16, r=r: t.transpose(
                        bA16[0:64, 512 + 128 * r:512 + 128 * r + 128], qtk_lo[:, ssl],
                        identb[:, :]))
                    ttrA = p.emit("t", lambda t, ssl=ssl, bA16=bA16, r=r: t.transpose(
                        bA16[64:128, 512 + 128 * r:512 + 128 * r + 128], qtk_lo[:, ssl],
                        identb[:, :]), inc="st" if r == 3 else None)
                    wtB = (list(qk_ev[(1, gl, g // 2)]) + [bank_free[bB]]) if r == 0 else []
                    p.emit("t", lambda t, ssl=ssl, bB16=bB16, r=r: t.transpose(
                        bB16[0:64, 128 * r:128 * r + 128], ktk_hi[:, ssl], identb[:, :]),
                        waits=wtB)
                    ttrB = p.emit("t", lambda t, ssl=ssl, bB16=bB16, r=r: t.transpose(
                        bB16[64:128, 128 * r:128 * r + 128], ktk_lo[:, ssl], identb[:, :]),
                        inc="st" if r == 3 else None)
                eng, sm = ("v", "sv") if g % 2 == 0 else ("s", "ss")
                t1 = p.emit(eng, lambda e, g=g, bA16=bA16: ecopy(e,
                    bass.AP(qp_hi, 4 * g, [[2048, 128], [1, 4], [16, 128]]),
                    bA16[:, 0:512]), waits=[ttrA], inc=sm)
                t2 = p.emit(eng, lambda e, g=g, bA16=bA16: ecopy(e,
                    bass.AP(qp_lo, 4 * g, [[2048, 128], [1, 4], [16, 128]]),
                    bA16[:, 512:1024]), inc=sm)
                bank_free[bA] = t2
                t3 = p.emit(eng, lambda e, g=g, bB16=bB16: ecopy(e,
                    bass.AP(kp, 4 * g, [[2048, 128], [1, 4], [16, 128]]),
                    bB16[:, 0:512]), waits=[ttrB], inc=sm)
                bank_free[bB] = t3
            tok_qkg = p.tok("sv")
            tok_qkg_s = p.tok("ss")

            def emit_S(kt, qb_):
                bS = 4 + (kt % 4)
                p.emit("t", lambda t, kt=kt, qb_=qb_, bS=bS: t.matmul(
                    pb[bS][:, :], kp[:, 128 * kt:128 * kt + 128],
                    qp_hi[:, 512 * qb_:512 * qb_ + 512], start=True, stop=False),
                    waits=[tok_qkg, tok_qkg_s, bank_free[bS]])
                tS = p.emit("t", lambda t, kt=kt, qb_=qb_, bS=bS: t.matmul(
                    pb[bS][:, :], kp[:, 128 * kt:128 * kt + 128],
                    qp_lo[:, 512 * qb_:512 * qb_ + 512], start=False, stop=True),
                    inc="st")
                if kt >= 4 * qb_:
                    j = kt - 4 * qb_
                    tS = p.emit("v", lambda v, bS=bS, j=j: v.tensor_tensor(
                        pb[bS][:, :], pb[bS][:, :],
                        maskT[:, 512 * j:512 * j + 512], OP.add),
                        waits=[tS], inc="sv")
                pslot = kt % 4
                tE = p.emit("s", lambda s, bS=bS, pslot=pslot: s.activation(
                    pt[pslot], pb[bS][:, :], AF.Exp,
                    bias=cvec[:, C_EXPB:C_EXPB + 1], scale=0.125),
                    waits=[tS, pt_guard[pslot]], inc="ss")
                bank_free[bS] = tE
                return tE

            pend = [None]
            for qb_ in range(4):
                nk = 4 * qb_ + 4
                bO = 2 + (qb_ % 2)
                bR = qb_ % 2
                tEs = {}
                for kk_ in range(min(3, nk)):
                    tEs[kk_] = emit_S(kk_, qb_)
                tok_pv = None
                for kt in range(nk):
                    if kt == 3 and pend[0] is not None:
                        pend[0]()
                        pend[0] = None
                    if kt + 3 < nk:
                        tEs[kt + 3] = emit_S(kt + 3, qb_)
                    pslot = kt % 4
                    w_pv = [tEs[kt], tok_vg]
                    if kt == 0:
                        w_pv.append(bank_free[bO])
                    tok_pv = p.emit("t", lambda t, kt=kt, pslot=pslot, nk=nk, bO=bO:
                                    t.matmul(
                        pb[bO][0:65, :], vgo[:, 65 * kt:65 * kt + 65], pt[pslot],
                        start=(kt == 0), stop=(kt == nk - 1)),
                        waits=w_pv, inc="st")
                    pt_guard[pslot] = tok_pv
                tden = p.emit("v", lambda v, bO=bO: v.reciprocal(
                    rr2[64:65, 0:512], pb[bO][64:65, :]),
                    waits=[tok_pv, rr_guard[0]], inc="sv")

                def mk_pend(qb_, bO, bR, tden):
                    def flush():
                        trb = p.emit("t", lambda t, bR=bR: t.matmul(
                            pb[bR][:, :], onest[64:65, 0:128], rr2[64:65, 0:512],
                            start=True, stop=True),
                            waits=[tden, bank_free[bR]], inc="st")
                        rr_guard[0] = trb
                        trbe = p.emit("s", lambda s, bR=bR: s.activation(
                            rb[:, :], pb[bR][:, :], AF.Identity,
                            bias=cvec[:, C_ZERO:C_ZERO + 1], scale=1.0),
                            waits=[trb, rb_guard[0]], inc="ss")
                        bank_free[bR] = trbe
                        tot = p.emit("v", lambda v, qb_=qb_, bO=bO: v.tensor_tensor(
                            ot[:, 512 * qb_:512 * qb_ + 512], pb[bO][0:64, :],
                            rb[0:64, :], OP.mult),
                            waits=[trbe, ot_guard[0]], inc="sv")
                        rb_guard[0] = tot
                        bank_free[bO] = tot
                    return flush

                pend[0] = mk_pend(qb_, bO, bR, tden)
            pend[0]()
            pend[0] = None
            tok_ot = p.tok("sv")

            for blk in range(2):
                b = blk
                p.emit("t", lambda t, blk=blk, b=b: t.matmul(
                    pb[b][:, :], eperm[:, 0:128],
                    bass.AP(ot, 8 * blk, [[2048, 64], [2, 4], [16, 128]]),
                    start=True, stop=False),
                    waits=[tok_ot, bank_free[b]])
                tui = p.emit("t", lambda t, blk=blk, b=b: t.matmul(
                    pb[b][:, :], eperm[:, 128:256],
                    bass.AP(ot, 8 * blk + 1, [[2048, 64], [2, 4], [16, 128]]),
                    start=False, stop=True),
                    inc="st")
                ot_guard[0] = tui
                eng, sm = ("v", "sv") if blk == 0 else ("s", "ss")
                tev = p.emit(eng, lambda e, blk=blk, b=b, gl=gl: ecopy(e,
                    bass.AP(aot, 1024 * blk + 128 * gl,
                            [[2048, 128], [256, 4], [1, 128]]),
                    pb[b][:, :]),
                    waits=[tui], inc=sm)
                bank_free[b] = tev
        tok_aot = p.tok("sv")
        tok_aot_s = p.tok("ss")

        kfw, tlf = load_w(fw2, h)
        tok_wqz, _ = quant(wb[kfw][:, :], svw[:, 0:1],
                           wqz[:, 1024 * h:1024 * h + 1024], [tlf, tok_swb])
        wb_free[kfw] = tok_wqz
        zn_guard[0] = tok_wqz if h == 7 else zn_guard[0]

        pr_done = []
        for c in range(8):
            kpw, tlp = load_w(pw2, c)
            for mt in range(2):
                for oh in range(2):
                    b = 2 * mt + oh
                    w_s = [tlp, tok_aot, tok_aot_s]
                    if c == 0:
                        w_s.append(bank_free[b])
                    last = (mt == 1 and oh == 1)
                    tm = p.emit("t", lambda t, c=c, mt=mt, oh=oh, b=b, kpw=kpw: t.matmul(
                        pb[b][:, :], aot[:, 256 * c + 128 * mt:256 * c + 128 * mt + 128],
                        wb[kpw][:, 512 * oh:512 * oh + 512],
                        start=(c == 0), stop=(c == 7)),
                        waits=w_s, inc="st" if last or c == 7 else None)
                    if c == 7:
                        pr_done.append((b, tm, mt, oh))
            wb_free[kpw] = p.tok("st")
        if h < 7:
            xnT_toks = emit_ln1_post(h + 1, emit_ln1_pre(h + 1),
                                     banks=(4, 5), evac_waits=[tok_v_mms])
        for mt in range(2):
            i = 2 * h + mt
            tz = []
            for (b, tmm, mt_, oh) in [x for x in pr_done if x[2] == mt]:
                tzz = p.emit("v", lambda v, b=b, oh=oh: v.scalar_tensor_tensor(
                    z0[:, 512 * oh:512 * oh + 512], pb[b][:, :], 1.0,
                    biasP[:, 512 * oh:512 * oh + 512], OP.mult, OP.add),
                    waits=[tmm, z0_guard[0]], inc="sv")
                bank_free[b] = tzz
                tz.append(tzz)
            wstat = layernorm_stats(z0[:, :], tz)
            tzn = p.emit("s", lambda s: s.activation(zn[:, :], z0[:, :], AF.Identity,
                                                     bias=nmr[:, 0:1], scale=rstd[:, 0:1]),
                         waits=[wstat, zn_guard[0]], inc="ss")
            ln_guard[0] = tzn
            z0_guard[0] = tzn
            q_guard[0] = tzn
            tadd = p.emit("v", lambda v, i=i: v.tensor_tensor(
                x_res[:, 1024 * i:1024 * (i + 1)], x_res[:, 1024 * i:1024 * (i + 1)],
                zn[:, :], OP.add),
                waits=[tzn], inc="sv")
            zn_guard[0] = tadd
            wstat = layernorm_stats(x_res[:, 1024 * i:1024 * (i + 1)], [tadd])
            p.emit("v", lambda v, i=i: v.tensor_copy(r3[:, i:i + 1], rstd[:, :]),
                   waits=[wstat], inc="sv")
            p.emit("v", lambda v, i=i: v.tensor_copy(nm3[:, i:i + 1], nmr[:, :]), inc="sv")
            p.emit("v", lambda v, i=i: v.tensor_reduce(
                mxs[:, i:i + 1], x_res[:, 1024 * i:1024 * (i + 1)], AX, OP.max), inc="sv")
            p.emit("v", lambda v, i=i: v.tensor_reduce(
                mns[:, i:i + 1], x_res[:, 1024 * i:1024 * (i + 1)], AX, OP.min), inc="sv")
            p.emit("v", lambda v, i=i: v.tensor_tensor(t_a[:, :], mxs[:, i:i + 1],
                                                       mv[:, 0:1], OP.subtract), inc="sv")
            p.emit("v", lambda v, i=i: v.tensor_tensor(t_b[:, :], mv[:, 0:1],
                                                       mns[:, i:i + 1], OP.subtract),
                   inc="sv")
            p.emit("v", lambda v: v.tensor_tensor(t_a[:, :], t_a[:, :], t_b[:, :], OP.max),
                   inc="sv")
            tra = p.emit("v", lambda v, i=i: v.tensor_tensor(ramax[:, i:i + 1], t_a[:, :],
                                                             rstd[:, :], OP.mult), inc="sv")
    tok_x2 = p.tok("sv")
    dump("d_x2", x_res[:, :], [tok_x2])
    dump("d_wqz", wqz[:, :], [tok_wqz])

    tam = p.emit("v", lambda v: v.tensor_reduce(wred[:, :], ramax[:, :], AX, OP.max),
                 waits=[tra], inc="sv")
    tam = p.emit("t", lambda t: t.transpose(pb[6][0:1, 0:128], wred[:, :], ident[:, :]),
                 waits=[tam, bank_free[6]], inc="st")
    tam = p.emit("v", lambda v: v.tensor_reduce(sc1[0:1, 0:1], pb[6][0:1, 0:128], AX,
                                                OP.max),
                 waits=[tam], inc="sv")
    bank_free[6] = tam
    tcc = p.emit("g", lambda g: g.dma_start(out=ccin.ap(), in_=sc1[0:1, 0:1]),
                 waits=[tam], inc="sg16", amt=16)
    tcc = p.emit("g", lambda g: g.collective_compute(
        "AllReduce", OP.max, replica_groups=[list(range(8))],
        ins=[ccin.ap().opt()], outs=[ccout.ap().opt()]),
        waits=[tcc], inc="scc")
    tcc = p.emit("g", lambda g: g.dma_start(out=sc1[0:1, 0:1], in_=ccout.ap()),
                 waits=[tcc], inc="sg16", amt=16)
    tsx = p.emit("v", lambda v: v.tensor_scalar(sc1[0:1, 0:1], sc1[0:1, 0:1], 1e-12, None,
                                                OP.max), waits=[tcc], inc="sv")
    tsx = p.emit("v", lambda v: v.reciprocal(sc1[0:1, 1:2], sc1[0:1, 0:1]),
                 waits=[tsx], inc="sv")
    tsx = p.emit("v", lambda v: v.tensor_scalar(sc1[0:1, 0:1], sc1[0:1, 1:2], 448.0, 448.0,
                                                OP.mult, OP.min), waits=[tsx], inc="sv")
    tix = p.emit("v", lambda v: v.reciprocal(sc1[0:1, 1:2], sc1[0:1, 0:1]),
                 waits=[tsx], inc="sv")
    tix = p.emit("v", lambda v: v.tensor_tensor(sc1[0:1, 1:2], sc1[0:1, 1:2], sc1[0:1, 3:4],
                                                OP.mult), waits=[tix], inc="sv")
    tbx = p.emit("t", lambda t: t.matmul(pb[6][:, 0:1], onest[0:1, 0:128], sc1[0:1, 0:1],
                                         start=True, stop=True),
                 waits=[tsx, bank_free[6]], inc="st")
    tbx2 = p.emit("v", lambda v: v.tensor_copy(svx[:, :], pb[6][:, 0:1]),
                  waits=[tbx], inc="sv")
    bank_free[6] = tbx2
    tbi = p.emit("t", lambda t: t.matmul(pb[7][:, 0:1], onest[0:1, 0:128], sc1[0:1, 1:2],
                                         start=True, stop=True),
                 waits=[tix, bank_free[7]], inc="st")
    tbi2 = p.emit("v", lambda v: v.tensor_copy(ixw[:, :], pb[7][:, 0:1]),
                  waits=[tbi], inc="sv")
    bank_free[7] = tbi2

    trlast_par = {0: None, 1: None}
    tq_par = {0: None, 1: None}
    o2sl = [ptA[:, 1024:1536].bitcast(BF16), ptA[:, 1536:2048].bitcast(BF16)]
    ffn_scr = [pt_guard[2], pt_guard[3], xnT_toks[0], xnT_toks[1]]
    o2_free = {0: list(ffn_scr), 1: list(ffn_scr)}
    lastF = [None]
    fdh = [None]
    xq_ev = {0: None, 1: None}

    def back_head_a(i):
        if i < 0 or i >= NT:
            return
        tq = tq_par[i % 2]
        tr_last = None
        for g in range(2):
            bk = 4 + g
            for r in range(4):
                c = 4 * g + r
                tr_last = p.emit("t", lambda t, c=c, bk=bk, r=r, i=i: t.transpose(
                    pb[bk][:, :].bitcast(BF16)[:, 128 * r:128 * r + 128],
                    xq[i % 2][:, 128 * c:128 * c + 128], identb[:, :]),
                    waits=[tq, bank_free[bk]] if r == 0 else [],
                    inc="st" if r == 3 else None)
            eng, sm = ("v", "sv") if g == 0 else ("s", "ss")
            tev = p.emit(eng, lambda e, bk=bk, g=g: ecopy(e,
                xqT[:, 512 * g:512 * g + 512],
                pb[bk][:, :].bitcast(BF16)[:, 0:512]),
                waits=[tr_last, lastF[0]], inc=sm)
            bank_free[bk] = tev
            xq_ev[g] = tev
        trlast_par[i % 2] = tr_last

    def back_head_b(i):
        if i < 0 or i >= NT:
            return
        tok_xqT = xq_ev[0]
        tok_xqT_s = xq_ev[1]
        f_done = []
        for oh in range(2):
            bk = oh
            for c in range(8):
                w_s = [tok_xqT, tok_xqT_s, tok_wqz]
                if c == 0:
                    w_s.append(bank_free[bk])
                tf = p.emit("t", lambda t, c=c, oh=oh, bk=bk: t.matmul(
                    pb[bk][:, :], xqT[:, 128 * c:128 * c + 128],
                    wqz[:, 1024 * c + 512 * oh:1024 * c + 512 * oh + 512],
                    start=(c == 0), stop=(c == 7)),
                    waits=w_s, inc="st" if c == 7 else None)
                if c == 7:
                    f_done.append((bk, tf, oh))
        lastF[0] = f_done[-1][1]
        fdh[0] = f_done

    def back_tail(j):
        if j < 0:
            return
        o2c = o2sl[j % 2]
        tz = []
        for (bk, tmm, oh) in fdh[0]:
            tzz = p.emit("v", lambda v, bk=bk, oh=oh, o2c=o2c: v.scalar_tensor_tensor(
                o2c[:, 512 * oh:512 * oh + 512], pb[bk][:, :], ixw[:, 0:1],
                biasF[:, 512 * oh:512 * oh + 512], OP.mult, OP.add),
                waits=[tmm] + list(o2_free[j % 2]), inc="sv")
            bank_free[bk] = tzz
            tz.append(tzz)
        wstat = layernorm_stats(o2c[:, :], tz)
        tzn = p.emit("s", lambda s, o2c=o2c: s.activation(
            zn[:, :], o2c[:, :], AF.Identity,
            bias=nmr[:, 0:1], scale=rstd[:, 0:1]),
            waits=[wstat, zn_guard[0]], inc="ss")
        ln_guard[0] = tzn
        o2_free[j % 2] = [tzn]
        tfin = p.emit("v", lambda v, j=j: v.tensor_tensor(
            x_res[:, 1024 * j:1024 * (j + 1)], x_res[:, 1024 * j:1024 * (j + 1)],
            zn[:, :], OP.add),
            waits=[tzn], inc="sv")
        zn_guard[0] = tfin
        p.emit("y", lambda y, j=j: y.dma_start(
            out=bass.AP(out, 128 * j * 1024, [[1024, 128], [1, 1024]]),
            in_=x_res[:, 1024 * j:1024 * (j + 1)]),
            waits=[tfin], inc="sout", amt=16)

    for ii in range(NT + 2):
        def mid(ii=ii):
            back_head_a(ii - 1)
            back_tail(ii - 2)
            back_head_b(ii - 1)
        if ii < NT:
            f = ii
            txn2 = p.emit("s", lambda s, f=f: s.activation(
                z0[:, :], x_res[:, 1024 * f:1024 * (f + 1)], AF.Identity,
                bias=nm3[:, f:f + 1], scale=r3[:, f:f + 1]),
                waits=[tbx2, tbi2, z0_guard[0], dgd("d_x2")], inc="ss")
            xq_guard[0] = trlast_par[f % 2]
            tq_par[f % 2], tq1 = quant(z0[:, :], svx[:, 0:1], xq[f % 2][:, :],
                                       [txn2], vbank=6, mid=mid)
            z0_guard[0] = tq1
        else:
            mid()

    p.emit("g", lambda g: g.nop(), waits=[("sout", 16 * NT)])

    with es, nc.Block() as block:
        p.bind(block)
    return nc


def make_consts():
    ident = np.eye(128, dtype=np.float32)
    identb = np.eye(128, dtype=np.float32).astype(ml_dtypes.bfloat16)
    onest = np.ones((128, 128), np.float32)
    eperm = np.zeros((64, 256), np.float32)
    for k in range(64):
        eperm[k, k] = 1.0
        eperm[k, 128 + 64 + k] = 1.0
    maskT = np.zeros((128, 2048), np.float32)
    tp = np.arange(128)
    tc = np.arange(512)
    for j in range(4):
        mpl = 8 * j + tp // 16
        ip = tp % 16
        ml = tc // 16
        iq = tc % 16
        m = (mpl[:, None] > ml[None, :]) | ((mpl[:, None] == ml[None, :]) &
                                            (ip[:, None] > iq[None, :]))
        maskT[:, 512 * j:512 * (j + 1)] = np.where(m, -30000.0, 0.0)
    cvec = np.zeros((128, 8), np.float32)
    cvec[:, C_EXPB] = -12.0
    cvec[:, C_EPS] = EPS
    cvec[:, C_MAG] = 24576.0
    cvec[:, C_NMAG] = -24576.0
    return ident, identb, onest, eperm, maskT.astype(ml_dtypes.float8_e5m2), cvec


_NC_CACHE = [None]


def kernel(L0_attn_x, L0_attn_Wq, L0_attn_Wk, L0_attn_Wv, L0_attn_proj_w,
           L0_attn_proj_b, L0_ffn_weight, L0_ffn_bias):
    x = np.ascontiguousarray(np.asarray(L0_attn_x, dtype=np.float32))
    Wq = np.asarray(L0_attn_Wq, np.float32)
    Wk = np.asarray(L0_attn_Wk, np.float32)
    Wv = np.asarray(L0_attn_Wv, np.float32)
    pw = np.asarray(L0_attn_proj_w, np.float32)
    pb_ = np.asarray(L0_attn_proj_b, np.float32)
    fw = np.asarray(L0_ffn_weight, np.float32)
    fb = np.asarray(L0_ffn_bias, np.float32)

    wq2 = np.ascontiguousarray(Wq.transpose(1, 0, 2).reshape(1024, 1024))
    wk2 = np.ascontiguousarray(Wk.transpose(1, 0, 2).reshape(1024, 1024))
    wv2 = np.ascontiguousarray(Wv.transpose(1, 0, 2).reshape(1024, 1024))
    pw2 = np.ascontiguousarray(pw.T)
    fw2 = np.ascontiguousarray(fw.T)
    biasP = np.ascontiguousarray(np.broadcast_to(pb_, (128, 1024)))
    biasF = np.ascontiguousarray(np.broadcast_to(fb, (128, 1024)))
    ident, identb, onest, eperm, maskT, cvec = make_consts()

    if _NC_CACHE[0] is None:
        _NC_CACHE[0] = build_nc()
    nc = _NC_CACHE[0]

    in_maps = []
    for c in range(8):
        in_maps.append(dict(
            x_in=np.ascontiguousarray(x[c]), wq2=wq2, wk2=wk2, wv2=wv2, pw2=pw2,
            fw2=fw2, biasP=biasP, biasF=biasF, maskT=maskT, eperm=eperm,
            ident=ident, identb=identb, onest=onest, cvec=cvec))
    res = run_bass_kernel_spmd(nc, in_maps, core_ids=list(range(8)))
    return np.stack([r["out"] for r in res.results], axis=0)


# version 23
# speedup vs baseline: 1.0504x; 1.0072x over previous
import os
import numpy as np
import ml_dtypes
import concourse.bass as bass
import concourse.mybir as mybir
from concourse.bass_utils import run_bass_kernel_spmd
from concourse.alu_op_type import AluOpType

DBG = os.environ.get("KDBG") == "1"
F32 = mybir.dt.float32
BF16 = mybir.dt.bfloat16
AF = mybir.ActivationFunctionType
AX = mybir.AxisListType.X
OP = AluOpType

NT = 16
EPS = 1e-5
C_EXPB, C_EPS, C_MAG, C_NMAG, C_ZERO = 0, 1, 2, 3, 6


class P:

    def __init__(self, nc, es):
        self.nc = nc
        self.es = es
        self.ops = {e: [] for e in ("t", "v", "s", "g", "y")}
        self.sems = {}
        self.cnt = {}

    def sem(self, name):
        self.sems[name] = self.es.enter_context(self.nc.semaphore(name))
        self.cnt[name] = 0

    def emit(self, eng, fn, waits=(), inc=None, amt=1):
        selfwait = None
        if eng == "v":
            selfwait = ("sv", self.cnt["sv"])
            if inc is None:
                inc = "sv"
        elif eng == "s":
            selfwait = ("ss", self.cnt["ss"])
            if inc is None:
                inc = "ss"
        tok = None
        if inc is not None:
            self.cnt[inc] += amt
            tok = (inc, self.cnt[inc])
        wl = [w for w in waits if w is not None]
        if selfwait is not None and selfwait[1] > 0:
            wl.append(selfwait)
        self.ops[eng].append((wl, fn, (inc, amt) if inc else None))
        return tok

    def tok(self, sem):
        return (sem, self.cnt[sem])

    def bind(self, block):
        sems = self.sems

        def runner(ops):
            def _(eng):
                hiw = {}
                for waits, fn, inc in ops:
                    for (s, v) in waits:
                        if hiw.get(s, -1) < v:
                            eng.wait_ge(sems[s], v)
                            hiw[s] = v
                    ins = fn(eng)
                    if inc is not None:
                        if ins is None:
                            eng.sem_inc(sems[inc[0]], inc[1])
                        else:
                            ins.then_inc(sems[inc[0]], inc[1])
            return _

        block.tensor(runner(self.ops["t"]))
        block.vector(runner(self.ops["v"]))
        block.scalar(runner(self.ops["s"]))
        block.gpsimd(runner(self.ops["g"]))
        block.sync(runner(self.ops["y"]))


def build_nc():
    from contextlib import ExitStack
    nc = bass.Bass("TRN2", target_bir_lowering=False, detect_race_conditions=False)

    x_in = nc.dram_tensor("x_in", [2048, 1024], F32, kind="ExternalInput")
    wq2 = nc.dram_tensor("wq2", [1024, 1024], F32, kind="ExternalInput")
    wk2 = nc.dram_tensor("wk2", [1024, 1024], F32, kind="ExternalInput")
    wv2 = nc.dram_tensor("wv2", [1024, 1024], F32, kind="ExternalInput")
    pw2 = nc.dram_tensor("pw2", [1024, 1024], F32, kind="ExternalInput")
    fw2 = nc.dram_tensor("fw2", [1024, 1024], F32, kind="ExternalInput")
    biasP_in = nc.dram_tensor("biasP", [128, 1024], F32, kind="ExternalInput")
    biasF_in = nc.dram_tensor("biasF", [128, 1024], F32, kind="ExternalInput")
    maskT_in = nc.dram_tensor("maskT", [128, 2048], mybir.dt.float8e5, kind="ExternalInput")
    eperm_in = nc.dram_tensor("eperm", [64, 256], F32, kind="ExternalInput")
    ident_in = nc.dram_tensor("ident", [128, 128], F32, kind="ExternalInput")
    identb_in = nc.dram_tensor("identb", [128, 128], BF16, kind="ExternalInput")
    onest_in = nc.dram_tensor("onest", [128, 128], F32, kind="ExternalInput")
    cvec_in = nc.dram_tensor("cvec", [128, 8], F32, kind="ExternalInput")
    out = nc.dram_tensor("out", [2048, 1024], F32, kind="ExternalOutput")
    dbg = {}
    if DBG:
        for nm, shp in (("d_xnT", [128, 2048]), ("d_x2", [128, 16384]),
                        ("d_wqz", [128, 8192])):
            dbg[nm] = nc.dram_tensor(nm, shp, BF16 if nm == "d_wqz" else F32,
                                     kind="ExternalOutput")
    ccin = nc.dram_tensor("ccin", [1, 1], F32)
    ccout = nc.dram_tensor("ccout", [1, 1], F32)

    es = ExitStack()
    S = lambda *a: es.enter_context(nc.sbuf_tensor(*a))

    x_res = S("x_res", [128, NT * 1024], F32)
    xnT = S("xnT", [128, 8 * 256], F32)
    qtk_hi = S("qtk_hi", [128, 2 * 1024], BF16)
    qtk_lo = S("qtk_lo", [128, 2 * 1024], BF16)
    ktk_hi = S("ktk_hi", [128, 2 * 1024], BF16)
    ktk_lo = S("ktk_lo", [128, 2 * 1024], BF16)
    vtok = S("vtok", [128, 2 * 1024], F32)
    qp_hi = S("qp_hi", [128, 2048], BF16)
    qp_lo = S("qp_lo", [128, 2048], BF16)
    kp = S("kp", [128, 2048], BF16)
    rr2 = S("rr2", [65, 512], F32)
    vgo = S("vgo", [128, 16 * 65], F32)
    ptA = S("ptA", [128, 4 * 512], F32)
    pt = [ptA[:, 512 * k:512 * k + 512] for k in range(4)]
    xn_sc = ptA[:, 1024:2048]
    ot = S("ot", [64, 2048], F32)
    rb = S("rb", [128, 512], F32)
    aot = S("aot", [128, 8 * 256], F32)
    z0 = S("z0", [128, 1024], F32)
    wb = [S(f"wb{k}", [128, 1024], F32) for k in range(2)]
    wqz = S("wqz", [128, 8 * 1024], BF16)
    xq = [S(f"xq{k}", [128, 1024], BF16) for k in range(2)]
    xqT = S("xqT", [128, 1024], BF16)
    qa = S("qa", [128, 1024], F32)
    zn = S("zn", [128, 1024], F32)
    qc = S("qc", [128, 1024], F32)
    qd = ptA[:, 0:512].bitcast(BF16)
    qm = S("qm", [128, 1024], mybir.dt.uint8)
    biasP = S("biasP_s", [128, 1024], F32)
    biasF = S("biasF_s", [128, 1024], F32)
    maskT = S("maskT_s", [128, 2048], mybir.dt.float8e5)
    eperm = S("eperm_s", [64, 256], F32)
    ident = S("ident_s", [128, 128], F32)
    identb = S("identb_s", [128, 128], BF16)
    onest = S("onest_s", [128, 128], F32)
    cvec = S("cvec_s", [128, 8], F32)
    bnst = S("bnst", [128, 12], F32)
    mv = S("mv", [128, 2], F32)
    t_a = S("t_a", [128, 1], F32)
    t_b = S("t_b", [128, 1], F32)
    rstd = S("rstd", [128, 1], F32)
    nmr = S("nmr", [128, 1], F32)
    r3 = S("r3", [128, NT], F32)
    nm3 = S("nm3", [128, NT], F32)
    mxs = S("mxs", [128, NT], F32)
    mns = S("mns", [128, NT], F32)
    ramax = S("ramax", [128, NT], F32)
    wA = S("wA", [128, 8], F32)
    wred = S("wred", [128, 1], F32)
    sc1 = S("sc1", [128, 4], F32)
    svx = S("svx", [128, 1], F32)
    svw = S("svw", [128, 1], F32)
    ixw = S("ixw", [128, 1], F32)

    pb = [es.enter_context(nc.psum_tensor(f"pb{k}", [128, 512], F32)) for k in range(8)]

    p = P(nc, es)
    for s in ("sy", "sw0", "sw1", "st", "sv", "ss", "sg16", "sgc", "scc", "sout"):
        p.sem(s)

    SW = ("sw0", "sw1")
    wb_free = [None, None]
    wslot = [0]

    def load_w(dram, c):
        k = wslot[0] % 2
        wslot[0] += 1
        src = bass.AP(dram, 128 * c * 1024, [[1024, 128], [1, 1024]])
        tok = p.emit("y", lambda y, s=src, k=k: y.dma_start(out=wb[k][:, :], in_=s),
                     waits=[wb_free[k]], inc=SW[k], amt=16)
        return k, tok

    def ecopy(e, dst, srcv):
        return e.tensor_copy(dst, srcv) if hasattr(e, "tensor_copy") else e.copy(dst, srcv)

    bank_free = [None] * 8
    ln_guard = [None]
    q_guard = [None]
    xq_guard = [None]
    pt_guard = {0: None, 1: None, 2: None, 3: None}
    ot_guard = [None]
    rb_guard = [None]
    rr_guard = [None]
    vg_guard = [None]
    zn_guard = [None]
    z0_guard = [None]

    dbw = {}

    def dump(nm, src_ap, waits):
        if DBG:
            dbw[nm] = p.emit("y", lambda y, nm=nm, s=src_ap: y.dma_start(
                out=dbg[nm].ap(), in_=s), waits=waits, inc="sout", amt=16)

    def dgd(nm):
        return dbw.get(nm)

    p.emit("y", lambda y: y.dma_start(
        out=bass.AP(x_res, 0, [[NT * 1024, 128], [1024, NT], [1, 1024]]),
        in_=bass.AP(x_in, 0, [[1024, 128], [131072, NT], [1, 1024]])), inc="sy", amt=16)
    for dst, src in ((biasP, biasP_in), (biasF, biasF_in), (maskT, maskT_in),
                     (eperm, eperm_in), (ident, ident_in), (identb, identb_in),
                     (onest, onest_in), (cvec, cvec_in)):
        p.emit("y", lambda y, d=dst, s=src: y.dma_start(out=d[:, :], in_=s.ap()),
               inc="sy", amt=16)
    tok_consts = p.tok("sy")
    tok_x = tok_consts

    tok_vones = p.emit("g", lambda g: g.memset(
        bass.AP(vgo, 64, [[1040, 128], [65, 16]]), 1.0), inc="sgc")

    for c in range(8):
        k, tokl = load_w(fw2, c)
        tokr = p.emit("v", lambda v, k=k, c=c: v.tensor_reduce(
            wA[:, c:c + 1], wb[k][:, :], AX, OP.max, apply_absolute_value=True),
            waits=[tokl], inc="sv")
        wb_free[k] = tokr
    tok = p.emit("v", lambda v: v.tensor_reduce(wred[:, :], wA[:, :], AX, OP.max), inc="sv")
    tok = p.emit("t", lambda t: t.transpose(pb[6][0:1, 0:128], wred[:, :], ident[:, :]),
                 waits=[tok, tok_consts, bank_free[6]], inc="st")
    tok = p.emit("v", lambda v: v.tensor_reduce(sc1[0:1, 2:3], pb[6][0:1, 0:128], AX, OP.max),
                 waits=[tok], inc="sv")
    bank_free[6] = tok
    tok = p.emit("v", lambda v: v.tensor_scalar(sc1[0:1, 2:3], sc1[0:1, 2:3], 1e-12, None,
                                                OP.max), waits=[tok], inc="sv")
    tok = p.emit("v", lambda v: v.reciprocal(sc1[0:1, 3:4], sc1[0:1, 2:3]),
                 waits=[tok], inc="sv")
    tok = p.emit("v", lambda v: v.tensor_scalar(sc1[0:1, 2:3], sc1[0:1, 3:4], 448.0, 448.0,
                                                OP.mult, OP.min), waits=[tok], inc="sv")
    tok_sw = p.emit("v", lambda v: v.reciprocal(sc1[0:1, 3:4], sc1[0:1, 2:3]),
                    waits=[tok], inc="sv")
    tok = p.emit("t", lambda t: t.matmul(pb[6][:, 0:1], onest[0:1, 0:128], sc1[0:1, 2:3],
                                         start=True, stop=True),
                 waits=[tok_sw, bank_free[6]], inc="st")
    tok_swb = p.emit("v", lambda v: v.tensor_copy(svw[:, :], pb[6][:, 0:1]),
                     waits=[tok], inc="sv")
    bank_free[6] = tok_swb

    def quant(src_ap, svec_col, dst_ap, waits, vbank=6, mid=None):
        t1 = p.emit("v", lambda v: v.tensor_scalar(qa[:, :], src_ap, svec_col, 448.0,
                                                   OP.mult, OP.min),
                    waits=list(waits) + [q_guard[0]], inc="sv")
        t2 = p.emit("v", lambda v: v.tensor_scalar(qa[:, :], qa[:, :], -448.0, None, OP.max),
                    waits=[t1], inc="sv")
        if mid is not None:
            mid()
        if True:
            vb2 = [pb[vbank][:, :], pb[vbank + 1][:, :]]
            t3a = p.emit("s", lambda s: s.activation(vb2[0], qa[:, 0:512], AF.Identity,
                                                     bias=cvec[:, C_ZERO:C_ZERO + 1],
                                                     scale=1048577.0),
                         waits=[t2, bank_free[vbank]], inc="ss")
            t3 = p.emit("s", lambda s: s.activation(vb2[1], qa[:, 512:1024], AF.Identity,
                                                    bias=cvec[:, C_ZERO:C_ZERO + 1],
                                                    scale=1048577.0),
                        waits=[bank_free[vbank + 1]], inc="ss")
            t4 = None
            for hf in range(2):
                t4 = p.emit("v", lambda v, hf=hf: v.tensor_tensor(
                    qc[:, 512 * hf:512 * hf + 512], vb2[hf], qa[:, 512 * hf:512 * hf + 512],
                    OP.subtract), waits=[t3a, t3], inc="sv")
            t5 = None
            for hf in range(2):
                t5 = p.emit("v", lambda v, hf=hf, dst_ap=dst_ap: v.tensor_tensor(
                    dst_ap[:, 512 * hf:512 * hf + 512], vb2[hf],
                    qc[:, 512 * hf:512 * hf + 512], OP.subtract),
                    waits=[t4, xq_guard[0]], inc="sv")
            bank_free[vbank] = t5
            bank_free[vbank + 1] = t5
        t6 = p.emit("s", lambda s: s.activation(qc[:, :], qa[:, :], AF.Identity,
                                                bias=cvec[:, C_MAG:C_MAG + 1], scale=1.0),
                    waits=[t2, t5], inc="ss")
        t7 = p.emit("s", lambda s: s.activation(qc[:, :], qc[:, :], AF.Identity,
                                                bias=cvec[:, C_NMAG:C_NMAG + 1], scale=1.0),
                    waits=[t6], inc="ss")
        t8 = p.emit("s", lambda s: s.activation(qd[:, :], qa[:, :], AF.Abs,
                                                bias=cvec[:, C_ZERO:C_ZERO + 1], scale=1.0),
                    waits=[t2, pt_guard[0]], inc="ss")
        t9 = p.emit("v", lambda v: v.tensor_scalar(qm[:, :], qd[:, :], 0.015625, None,
                                                   OP.is_lt),
                    waits=[t8], inc="sv")
        pt_guard[0] = t9
        t10 = p.emit("v", lambda v, dst_ap=dst_ap: v.copy_predicated(
            dst_ap, qm[:, :], qc[:, :]), waits=[t5, t7, t9], inc="sv")
        q_guard[0] = t8
        return t10, t1

    def layernorm_stats(src_ap, waits):
        w1 = p.emit("v", lambda v: v.bn_stats(bnst[:, 0:6], src_ap[:, 0:512]),
                    waits=list(waits), inc="sv")
        p.emit("v", lambda v: v.bn_stats(bnst[:, 6:12], src_ap[:, 512:1024]), inc="sv")
        w3 = p.emit("v", lambda v: v.bn_aggr(mv[:, :], bnst[:, :]),
                    waits=[ln_guard[0]], inc="sv")
        w4 = p.emit("s", lambda s: s.activation(t_a[:, :], mv[:, 1:2], AF.Sqrt,
                                                bias=cvec[:, C_EPS:C_EPS + 1], scale=1.0),
                    waits=[w3, tok_consts], inc="ss")
        ln_guard[0] = w4
        w5 = p.emit("v", lambda v: v.reciprocal(rstd[:, :], t_a[:, :]), waits=[w4], inc="sv")
        p.emit("v", lambda v: v.tensor_scalar(t_b[:, :], mv[:, 1:2], EPS, None, OP.add),
               inc="sv")
        p.emit("v", lambda v: v.tensor_tensor(t_a[:, :], rstd[:, :], rstd[:, :], OP.mult),
               inc="sv")
        p.emit("v", lambda v: v.tensor_tensor(t_a[:, :], t_a[:, :], t_b[:, :], OP.mult),
               inc="sv")
        p.emit("v", lambda v: v.tensor_scalar(t_a[:, :], t_a[:, :], -0.5, 1.5,
                                              OP.mult, OP.add), inc="sv")
        p.emit("v", lambda v: v.tensor_tensor(rstd[:, :], rstd[:, :], t_a[:, :], OP.mult),
               inc="sv")
        p.emit("v", lambda v: v.tensor_tensor(nmr[:, :], mv[:, 0:1], rstd[:, :], OP.mult),
               inc="sv")
        w12 = p.emit("v", lambda v: v.tensor_scalar(nmr[:, :], nmr[:, :], -1.0, None,
                                                    OP.mult), inc="sv")
        return w12

    def emit_ln1_pre(h):
        i0 = 2 * h
        w0 = layernorm_stats(x_res[:, 1024 * i0:1024 * (i0 + 1)], [tok_x])
        txn0 = p.emit("s", lambda s, i0=i0: s.activation(
            z0[:, :], x_res[:, 1024 * i0:1024 * (i0 + 1)], AF.Identity,
            bias=nmr[:, 0:1], scale=rstd[:, 0:1]),
            waits=[w0, z0_guard[0]], inc="ss")
        ln_guard[0] = txn0
        w1 = layernorm_stats(x_res[:, 1024 * (i0 + 1):1024 * (i0 + 2)], [tok_x])
        txn1 = p.emit("s", lambda s, i0=i0: s.activation(
            zn[:, :], x_res[:, 1024 * (i0 + 1):1024 * (i0 + 2)], AF.Identity,
            bias=nmr[:, 0:1], scale=rstd[:, 0:1]),
            waits=[w1, zn_guard[0]], inc="ss")
        ln_guard[0] = txn1
        return txn0, txn1

    def emit_ln1_post(h, pre, banks=(4, 5), evac_waits=()):
        txn0, txn1 = pre
        for il in range(2):
            txn = txn0 if il == 0 else txn1
            srcb = z0 if il == 0 else zn
            for g in range(2):
                b = banks[g]
                ttr = None
                for r in range(4):
                    c = 4 * g + r
                    ttr = p.emit("t", lambda t, c=c, b=b, r=r, srcb=srcb: t.transpose(
                        pb[b][:, 128 * r:128 * r + 128],
                        srcb[:, 128 * c:128 * c + 128], ident[:, :]),
                        waits=[txn, bank_free[b]] if r == 0 else [],
                        inc="st" if r == 3 else None)
                eng, sm = ("v", "sv") if g == 0 else ("s", "ss")
                tev = p.emit(eng, lambda e, b=b, g=g, il=il: ecopy(e,
                    bass.AP(xnT, 1024 * g + 128 * il, [[2048, 128], [256, 4], [1, 128]]),
                    pb[b][:, :]),
                    waits=[ttr, dgd("d_xnT")] + list(evac_waits), inc=sm)
                bank_free[b] = tev
                if il == 0:
                    z0_guard[0] = ttr
                else:
                    zn_guard[0] = ttr
        return p.tok("sv"), p.tok("ss")

    xnT_toks = emit_ln1_post(0, emit_ln1_pre(0))

    for h in range(8):
        tok_xnT, tok_xnT_s = xnT_toks

        qk_done = []
        for c in range(8):
            kq, tlq = load_w(wq2, c)
            kk, tlk = load_w(wk2, c)
            for mat in range(2):
                kslot = kq if mat == 0 else kk
                tl = tlq if mat == 0 else tlk
                for mt in range(2):
                    for oh in range(2):
                        b = 4 * mat + 2 * mt + oh
                        w_s = [tl, tok_xnT, tok_xnT_s]
                        if c == 0:
                            w_s.append(bank_free[b])
                        last = (mt == 1 and oh == 1)
                        tm = p.emit("t", lambda t, c=c, mt=mt, oh=oh, b=b, ks=kslot:
                                    t.matmul(
                            pb[b][:, :],
                            xnT[:, 256 * c + 128 * mt:256 * c + 128 * mt + 128],
                            wb[ks][:, 512 * oh:512 * oh + 512],
                            start=(c == 0), stop=(c == 7)),
                            waits=w_s, inc="st" if last or c == 7 else None)
                        if c == 7:
                            qk_done.append((b, tm, mat, mt, oh))
                wb_free[kslot] = p.tok("st")
        qk_ev = {}
        for (b, tmm, mat, mt, oh) in qk_done:
            dh = qtk_hi if mat == 0 else ktk_hi
            dl = qtk_lo if mat == 0 else ktk_lo
            sl = slice(1024 * mt + 512 * oh, 1024 * mt + 512 * oh + 512)
            eng, sm = ("v", "sv") if b % 2 == 0 else ("s", "ss")
            th = p.emit(eng, lambda e, b=b, dh=dh, sl=sl: ecopy(e, dh[:, sl], pb[b][:, :]),
                        waits=[tmm], inc=sm)
            tev = p.emit("v", lambda v, b=b, dh=dh, dl=dl, sl=sl: v.tensor_tensor(
                dl[:, sl], pb[b][:, :], dh[:, sl], OP.subtract),
                waits=[th], inc="sv")
            bank_free[b] = tev
            qk_ev[(mat, mt, oh)] = (th, tev)
        tok_qk_ev = p.tok("sv")
        tok_qk_es = p.tok("ss")

        v_done = []
        for c in range(8):
            kv, tlv = load_w(wv2, c)
            for mt in range(2):
                for oh in range(2):
                    b = 2 * mt + oh
                    w_s = [tlv]
                    if c == 0:
                        w_s.append(bank_free[b])
                    last = (mt == 1 and oh == 1)
                    tm = p.emit("t", lambda t, c=c, mt=mt, oh=oh, b=b, kv=kv: t.matmul(
                        pb[b][:, :],
                        xnT[:, 256 * c + 128 * mt:256 * c + 128 * mt + 128],
                        wb[kv][:, 512 * oh:512 * oh + 512],
                        start=(c == 0), stop=(c == 7)),
                        waits=w_s, inc="st" if last or c == 7 else None)
                    if c == 7:
                        v_done.append((b, tm, mt, oh))
            wb_free[kv] = p.tok("st")
        tok_v_mms = p.tok("st")
        for (b, tmm, mt, oh) in v_done:
            eng, sm = ("v", "sv") if b % 2 == 0 else ("s", "ss")
            tev = p.emit(eng, lambda e, b=b, mt=mt, oh=oh: ecopy(e,
                vtok[:, 1024 * mt + 512 * oh:1024 * mt + 512 * oh + 512], pb[b][:, :]),
                waits=[tmm, vg_guard[0]], inc=sm)
            bank_free[b] = tev
        tok_v_ev = p.tok("sv")
        tok_v_es = p.tok("ss")

        for gl in range(2):
            gsp = gl
            gw = [tok_v_ev, tok_v_es, tok_vones, bank_free[2], bank_free[3]]
            for j in range(16):
                vg_guard[0] = p.emit("g", lambda g, j=j, gl=gl: g.dma_start(
                    out=bass.AP(vgo, 65 * j, [[1040, 128], [1, 64]]),
                    in_=bass.AP(vtok, (8 * j) * 2048 + 1024 * gl,
                                [[2048, 8], [1, 1024]])),
                    waits=gw if j == 0 else [], inc="sg16", amt=16)
            tok_vg = vg_guard[0]
            for g in range(4):
                bA = 4 + 2 * (g % 2)
                bB = bA + 1
                bA16 = pb[bA][:, :].bitcast(BF16)
                bB16 = pb[bB][:, :].bitcast(BF16)
                ttrA = ttrB = None
                for r in range(4):
                    i = 4 * g + r
                    ssl = slice(1024 * gl + 64 * i, 1024 * gl + 64 * i + 64)
                    wtA = (list(qk_ev[(0, gl, g // 2)]) + [bank_free[bA]]) if r == 0 else []
                    p.emit("t", lambda t, ssl=ssl, bA16=bA16, r=r: t.transpose(
                        bA16[0:64, 128 * r:128 * r + 128], qtk_hi[:, ssl], identb[:, :]),
                        waits=wtA)
                    p.emit("t", lambda t, ssl=ssl, bA16=bA16, r=r: t.transpose(
                        bA16[64:128, 128 * r:128 * r + 128], qtk_hi[:, ssl], identb[:, :]))
                    p.emit("t", lambda t, ssl=ssl, bA16=bA16, r=r: t.transpose(
                        bA16[0:64, 512 + 128 * r:512 + 128 * r + 128], qtk_lo[:, ssl],
                        identb[:, :]))
                    ttrA = p.emit("t", lambda t, ssl=ssl, bA16=bA16, r=r: t.transpose(
                        bA16[64:128, 512 + 128 * r:512 + 128 * r + 128], qtk_lo[:, ssl],
                        identb[:, :]), inc="st" if r == 3 else None)
                    wtB = (list(qk_ev[(1, gl, g // 2)]) + [bank_free[bB]]) if r == 0 else []
                    p.emit("t", lambda t, ssl=ssl, bB16=bB16, r=r: t.transpose(
                        bB16[0:64, 128 * r:128 * r + 128], ktk_hi[:, ssl], identb[:, :]),
                        waits=wtB)
                    ttrB = p.emit("t", lambda t, ssl=ssl, bB16=bB16, r=r: t.transpose(
                        bB16[64:128, 128 * r:128 * r + 128], ktk_lo[:, ssl], identb[:, :]),
                        inc="st" if r == 3 else None)
                eng, sm = ("v", "sv") if g % 2 == 0 else ("s", "ss")
                t1 = p.emit(eng, lambda e, g=g, bA16=bA16: ecopy(e,
                    bass.AP(qp_hi, 4 * g, [[2048, 128], [1, 4], [16, 128]]),
                    bA16[:, 0:512]), waits=[ttrA], inc=sm)
                t2 = p.emit(eng, lambda e, g=g, bA16=bA16: ecopy(e,
                    bass.AP(qp_lo, 4 * g, [[2048, 128], [1, 4], [16, 128]]),
                    bA16[:, 512:1024]), inc=sm)
                bank_free[bA] = t2
                t3 = p.emit(eng, lambda e, g=g, bB16=bB16: ecopy(e,
                    bass.AP(kp, 4 * g, [[2048, 128], [1, 4], [16, 128]]),
                    bB16[:, 0:512]), waits=[ttrB], inc=sm)
                bank_free[bB] = t3
            tok_qkg = p.tok("sv")
            tok_qkg_s = p.tok("ss")

            def emit_S(kt, qb_):
                bS = 4 + (kt % 4)
                p.emit("t", lambda t, kt=kt, qb_=qb_, bS=bS: t.matmul(
                    pb[bS][:, :], kp[:, 128 * kt:128 * kt + 128],
                    qp_hi[:, 512 * qb_:512 * qb_ + 512], start=True, stop=False),
                    waits=[tok_qkg, tok_qkg_s, bank_free[bS]])
                tS = p.emit("t", lambda t, kt=kt, qb_=qb_, bS=bS: t.matmul(
                    pb[bS][:, :], kp[:, 128 * kt:128 * kt + 128],
                    qp_lo[:, 512 * qb_:512 * qb_ + 512], start=False, stop=True),
                    inc="st")
                if kt >= 4 * qb_:
                    j = kt - 4 * qb_
                    tS = p.emit("v", lambda v, bS=bS, j=j: v.tensor_tensor(
                        pb[bS][:, :], pb[bS][:, :],
                        maskT[:, 512 * j:512 * j + 512], OP.add),
                        waits=[tS], inc="sv")
                pslot = kt % 4
                tE = p.emit("s", lambda s, bS=bS, pslot=pslot: s.activation(
                    pt[pslot], pb[bS][:, :], AF.Exp,
                    bias=cvec[:, C_EXPB:C_EXPB + 1], scale=0.125),
                    waits=[tS, pt_guard[pslot]], inc="ss")
                bank_free[bS] = tE
                return tE

            pend = [None]
            for qb_ in range(4):
                nk = 4 * qb_ + 4
                bO = 2 + (qb_ % 2)
                bR = qb_ % 2
                tEs = {}
                for kk_ in range(min(3, nk)):
                    tEs[kk_] = emit_S(kk_, qb_)
                tok_pv = None
                for kt in range(nk):
                    if kt == min(4, nk - 1) and pend[0] is not None:
                        pend[0]()
                        pend[0] = None
                    if kt + 3 < nk:
                        tEs[kt + 3] = emit_S(kt + 3, qb_)
                    pslot = kt % 4
                    w_pv = [tEs[kt], tok_vg]
                    if kt == 0:
                        w_pv.append(bank_free[bO])
                    tok_pv = p.emit("t", lambda t, kt=kt, pslot=pslot, nk=nk, bO=bO:
                                    t.matmul(
                        pb[bO][0:65, :], vgo[:, 65 * kt:65 * kt + 65], pt[pslot],
                        start=(kt == 0), stop=(kt == nk - 1)),
                        waits=w_pv, inc="st")
                    pt_guard[pslot] = tok_pv
                tden = p.emit("v", lambda v, bO=bO: v.reciprocal(
                    rr2[64:65, 0:512], pb[bO][64:65, :]),
                    waits=[tok_pv, rr_guard[0]], inc="sv")

                def mk_pend(qb_, bO, bR, tden):
                    def flush():
                        trb = p.emit("t", lambda t, bR=bR: t.matmul(
                            pb[bR][:, :], onest[64:65, 0:128], rr2[64:65, 0:512],
                            start=True, stop=True),
                            waits=[tden, bank_free[bR]], inc="st")
                        rr_guard[0] = trb
                        trbe = p.emit("s", lambda s, bR=bR: s.activation(
                            rb[:, :], pb[bR][:, :], AF.Identity,
                            bias=cvec[:, C_ZERO:C_ZERO + 1], scale=1.0),
                            waits=[trb, rb_guard[0]], inc="ss")
                        bank_free[bR] = trbe
                        tot = p.emit("v", lambda v, qb_=qb_, bO=bO: v.tensor_tensor(
                            ot[:, 512 * qb_:512 * qb_ + 512], pb[bO][0:64, :],
                            rb[0:64, :], OP.mult),
                            waits=[trbe, ot_guard[0]], inc="sv")
                        rb_guard[0] = tot
                        bank_free[bO] = tot
                    return flush

                pend[0] = mk_pend(qb_, bO, bR, tden)
            pend[0]()
            pend[0] = None
            tok_ot = p.tok("sv")

            for blk in range(2):
                b = blk
                p.emit("t", lambda t, blk=blk, b=b: t.matmul(
                    pb[b][:, :], eperm[:, 0:128],
                    bass.AP(ot, 8 * blk, [[2048, 64], [2, 4], [16, 128]]),
                    start=True, stop=False),
                    waits=[tok_ot, bank_free[b]])
                tui = p.emit("t", lambda t, blk=blk, b=b: t.matmul(
                    pb[b][:, :], eperm[:, 128:256],
                    bass.AP(ot, 8 * blk + 1, [[2048, 64], [2, 4], [16, 128]]),
                    start=False, stop=True),
                    inc="st")
                ot_guard[0] = tui
                eng, sm = ("v", "sv") if blk == 0 else ("s", "ss")
                tev = p.emit(eng, lambda e, blk=blk, b=b, gl=gl: ecopy(e,
                    bass.AP(aot, 1024 * blk + 128 * gl,
                            [[2048, 128], [256, 4], [1, 128]]),
                    pb[b][:, :]),
                    waits=[tui], inc=sm)
                bank_free[b] = tev
        tok_aot = p.tok("sv")
        tok_aot_s = p.tok("ss")

        kfw, tlf = load_w(fw2, h)
        tok_wqz, _ = quant(wb[kfw][:, :], svw[:, 0:1],
                           wqz[:, 1024 * h:1024 * h + 1024], [tlf, tok_swb])
        wb_free[kfw] = tok_wqz
        zn_guard[0] = tok_wqz if h == 7 else zn_guard[0]

        pr_done = []
        for c in range(8):
            kpw, tlp = load_w(pw2, c)
            for mt in range(2):
                for oh in range(2):
                    b = 2 * mt + oh
                    w_s = [tlp, tok_aot, tok_aot_s]
                    if c == 0:
                        w_s.append(bank_free[b])
                    last = (mt == 1 and oh == 1)
                    tm = p.emit("t", lambda t, c=c, mt=mt, oh=oh, b=b, kpw=kpw: t.matmul(
                        pb[b][:, :], aot[:, 256 * c + 128 * mt:256 * c + 128 * mt + 128],
                        wb[kpw][:, 512 * oh:512 * oh + 512],
                        start=(c == 0), stop=(c == 7)),
                        waits=w_s, inc="st" if last or c == 7 else None)
                    if c == 7:
                        pr_done.append((b, tm, mt, oh))
            wb_free[kpw] = p.tok("st")
        if h < 7:
            xnT_toks = emit_ln1_post(h + 1, emit_ln1_pre(h + 1),
                                     banks=(4, 5), evac_waits=[tok_v_mms])
        for mt in range(2):
            i = 2 * h + mt
            tz = []
            for (b, tmm, mt_, oh) in [x for x in pr_done if x[2] == mt]:
                tzz = p.emit("v", lambda v, b=b, oh=oh: v.scalar_tensor_tensor(
                    z0[:, 512 * oh:512 * oh + 512], pb[b][:, :], 1.0,
                    biasP[:, 512 * oh:512 * oh + 512], OP.mult, OP.add),
                    waits=[tmm, z0_guard[0]], inc="sv")
                bank_free[b] = tzz
                tz.append(tzz)
            wstat = layernorm_stats(z0[:, :], tz)
            tzn = p.emit("s", lambda s: s.activation(zn[:, :], z0[:, :], AF.Identity,
                                                     bias=nmr[:, 0:1], scale=rstd[:, 0:1]),
                         waits=[wstat, zn_guard[0]], inc="ss")
            ln_guard[0] = tzn
            z0_guard[0] = tzn
            q_guard[0] = tzn
            tadd = p.emit("v", lambda v, i=i: v.tensor_tensor(
                x_res[:, 1024 * i:1024 * (i + 1)], x_res[:, 1024 * i:1024 * (i + 1)],
                zn[:, :], OP.add),
                waits=[tzn], inc="sv")
            zn_guard[0] = tadd
            wstat = layernorm_stats(x_res[:, 1024 * i:1024 * (i + 1)], [tadd])
            p.emit("v", lambda v, i=i: v.tensor_copy(r3[:, i:i + 1], rstd[:, :]),
                   waits=[wstat], inc="sv")
            p.emit("v", lambda v, i=i: v.tensor_copy(nm3[:, i:i + 1], nmr[:, :]), inc="sv")
            p.emit("v", lambda v, i=i: v.tensor_reduce(
                mxs[:, i:i + 1], x_res[:, 1024 * i:1024 * (i + 1)], AX, OP.max), inc="sv")
            p.emit("v", lambda v, i=i: v.tensor_reduce(
                mns[:, i:i + 1], x_res[:, 1024 * i:1024 * (i + 1)], AX, OP.min), inc="sv")
            p.emit("v", lambda v, i=i: v.tensor_tensor(t_a[:, :], mxs[:, i:i + 1],
                                                       mv[:, 0:1], OP.subtract), inc="sv")
            p.emit("v", lambda v, i=i: v.tensor_tensor(t_b[:, :], mv[:, 0:1],
                                                       mns[:, i:i + 1], OP.subtract),
                   inc="sv")
            p.emit("v", lambda v: v.tensor_tensor(t_a[:, :], t_a[:, :], t_b[:, :], OP.max),
                   inc="sv")
            tra = p.emit("v", lambda v, i=i: v.tensor_tensor(ramax[:, i:i + 1], t_a[:, :],
                                                             rstd[:, :], OP.mult), inc="sv")
    tok_x2 = p.tok("sv")
    dump("d_x2", x_res[:, :], [tok_x2])
    dump("d_wqz", wqz[:, :], [tok_wqz])

    tam = p.emit("v", lambda v: v.tensor_reduce(wred[:, :], ramax[:, :], AX, OP.max),
                 waits=[tra], inc="sv")
    tam = p.emit("t", lambda t: t.transpose(pb[6][0:1, 0:128], wred[:, :], ident[:, :]),
                 waits=[tam, bank_free[6]], inc="st")
    tam = p.emit("v", lambda v: v.tensor_reduce(sc1[0:1, 0:1], pb[6][0:1, 0:128], AX,
                                                OP.max),
                 waits=[tam], inc="sv")
    bank_free[6] = tam
    tcc = p.emit("g", lambda g: g.dma_start(out=ccin.ap(), in_=sc1[0:1, 0:1]),
                 waits=[tam], inc="sg16", amt=16)
    tcc = p.emit("g", lambda g: g.collective_compute(
        "AllReduce", OP.max, replica_groups=[list(range(8))],
        ins=[ccin.ap().opt()], outs=[ccout.ap().opt()]),
        waits=[tcc], inc="scc")
    tcc = p.emit("g", lambda g: g.dma_start(out=sc1[0:1, 0:1], in_=ccout.ap()),
                 waits=[tcc], inc="sg16", amt=16)
    tsx = p.emit("v", lambda v: v.tensor_scalar(sc1[0:1, 0:1], sc1[0:1, 0:1], 1e-12, None,
                                                OP.max), waits=[tcc], inc="sv")
    tsx = p.emit("v", lambda v: v.reciprocal(sc1[0:1, 1:2], sc1[0:1, 0:1]),
                 waits=[tsx], inc="sv")
    tsx = p.emit("v", lambda v: v.tensor_scalar(sc1[0:1, 0:1], sc1[0:1, 1:2], 448.0, 448.0,
                                                OP.mult, OP.min), waits=[tsx], inc="sv")
    tix = p.emit("v", lambda v: v.reciprocal(sc1[0:1, 1:2], sc1[0:1, 0:1]),
                 waits=[tsx], inc="sv")
    tix = p.emit("v", lambda v: v.tensor_tensor(sc1[0:1, 1:2], sc1[0:1, 1:2], sc1[0:1, 3:4],
                                                OP.mult), waits=[tix], inc="sv")
    tbx = p.emit("t", lambda t: t.matmul(pb[6][:, 0:1], onest[0:1, 0:128], sc1[0:1, 0:1],
                                         start=True, stop=True),
                 waits=[tsx, bank_free[6]], inc="st")
    tbx2 = p.emit("v", lambda v: v.tensor_copy(svx[:, :], pb[6][:, 0:1]),
                  waits=[tbx], inc="sv")
    bank_free[6] = tbx2
    tbi = p.emit("t", lambda t: t.matmul(pb[7][:, 0:1], onest[0:1, 0:128], sc1[0:1, 1:2],
                                         start=True, stop=True),
                 waits=[tix, bank_free[7]], inc="st")
    tbi2 = p.emit("v", lambda v: v.tensor_copy(ixw[:, :], pb[7][:, 0:1]),
                  waits=[tbi], inc="sv")
    bank_free[7] = tbi2

    trlast_par = {0: None, 1: None}
    tq_par = {0: None, 1: None}
    o2sl = [ptA[:, 1024:1536].bitcast(BF16), ptA[:, 1536:2048].bitcast(BF16)]
    ffn_scr = [pt_guard[2], pt_guard[3], xnT_toks[0], xnT_toks[1]]
    o2_free = {0: list(ffn_scr), 1: list(ffn_scr)}
    lastF = [None]
    fdh = [None]
    xq_ev = {0: None, 1: None}

    def back_head_a(i):
        if i < 0 or i >= NT:
            return
        tq = tq_par[i % 2]
        tr_last = None
        for g in range(2):
            bk = 4 + g
            for r in range(4):
                c = 4 * g + r
                tr_last = p.emit("t", lambda t, c=c, bk=bk, r=r, i=i: t.transpose(
                    pb[bk][:, :].bitcast(BF16)[:, 128 * r:128 * r + 128],
                    xq[i % 2][:, 128 * c:128 * c + 128], identb[:, :]),
                    waits=[tq, bank_free[bk]] if r == 0 else [],
                    inc="st" if r == 3 else None)
            eng, sm = ("v", "sv") if g == 0 else ("s", "ss")
            tev = p.emit(eng, lambda e, bk=bk, g=g: ecopy(e,
                xqT[:, 512 * g:512 * g + 512],
                pb[bk][:, :].bitcast(BF16)[:, 0:512]),
                waits=[tr_last, lastF[0]], inc=sm)
            bank_free[bk] = tev
            xq_ev[g] = tev
        trlast_par[i % 2] = tr_last

    def back_head_b(i):
        if i < 0 or i >= NT:
            return
        tok_xqT = xq_ev[0]
        tok_xqT_s = xq_ev[1]
        f_done = []
        for oh in range(2):
            bk = oh
            for c in range(8):
                w_s = [tok_xqT, tok_xqT_s, tok_wqz]
                if c == 0:
                    w_s.append(bank_free[bk])
                tf = p.emit("t", lambda t, c=c, oh=oh, bk=bk: t.matmul(
                    pb[bk][:, :], xqT[:, 128 * c:128 * c + 128],
                    wqz[:, 1024 * c + 512 * oh:1024 * c + 512 * oh + 512],
                    start=(c == 0), stop=(c == 7)),
                    waits=w_s, inc="st" if c == 7 else None)
                if c == 7:
                    f_done.append((bk, tf, oh))
        lastF[0] = f_done[-1][1]
        fdh[0] = f_done

    def back_tail(j):
        if j < 0:
            return
        o2c = o2sl[j % 2]
        tz = []
        for (bk, tmm, oh) in fdh[0]:
            tzz = p.emit("v", lambda v, bk=bk, oh=oh, o2c=o2c: v.scalar_tensor_tensor(
                o2c[:, 512 * oh:512 * oh + 512], pb[bk][:, :], ixw[:, 0:1],
                biasF[:, 512 * oh:512 * oh + 512], OP.mult, OP.add),
                waits=[tmm] + list(o2_free[j % 2]), inc="sv")
            bank_free[bk] = tzz
            tz.append(tzz)
        wstat = layernorm_stats(o2c[:, :], tz)
        tzn = p.emit("s", lambda s, o2c=o2c: s.activation(
            zn[:, :], o2c[:, :], AF.Identity,
            bias=nmr[:, 0:1], scale=rstd[:, 0:1]),
            waits=[wstat, zn_guard[0]], inc="ss")
        ln_guard[0] = tzn
        o2_free[j % 2] = [tzn]
        tfin = p.emit("v", lambda v, j=j: v.tensor_tensor(
            x_res[:, 1024 * j:1024 * (j + 1)], x_res[:, 1024 * j:1024 * (j + 1)],
            zn[:, :], OP.add),
            waits=[tzn], inc="sv")
        zn_guard[0] = tfin
        p.emit("y", lambda y, j=j: y.dma_start(
            out=bass.AP(out, 128 * j * 1024, [[1024, 128], [1, 1024]]),
            in_=x_res[:, 1024 * j:1024 * (j + 1)]),
            waits=[tfin], inc="sout", amt=16)

    for ii in range(NT + 2):
        def mid(ii=ii):
            back_head_a(ii - 1)
            back_tail(ii - 2)
            back_head_b(ii - 1)
        if ii < NT:
            f = ii
            txn2 = p.emit("s", lambda s, f=f: s.activation(
                z0[:, :], x_res[:, 1024 * f:1024 * (f + 1)], AF.Identity,
                bias=nm3[:, f:f + 1], scale=r3[:, f:f + 1]),
                waits=[tbx2, tbi2, z0_guard[0], dgd("d_x2")], inc="ss")
            xq_guard[0] = trlast_par[f % 2]
            tq_par[f % 2], tq1 = quant(z0[:, :], svx[:, 0:1], xq[f % 2][:, :],
                                       [txn2], vbank=6, mid=mid)
            z0_guard[0] = tq1
        else:
            mid()

    p.emit("g", lambda g: g.nop(), waits=[("sout", 16 * NT)])

    with es, nc.Block() as block:
        p.bind(block)
    return nc


def make_consts():
    ident = np.eye(128, dtype=np.float32)
    identb = np.eye(128, dtype=np.float32).astype(ml_dtypes.bfloat16)
    onest = np.ones((128, 128), np.float32)
    eperm = np.zeros((64, 256), np.float32)
    for k in range(64):
        eperm[k, k] = 1.0
        eperm[k, 128 + 64 + k] = 1.0
    maskT = np.zeros((128, 2048), np.float32)
    tp = np.arange(128)
    tc = np.arange(512)
    for j in range(4):
        mpl = 8 * j + tp // 16
        ip = tp % 16
        ml = tc // 16
        iq = tc % 16
        m = (mpl[:, None] > ml[None, :]) | ((mpl[:, None] == ml[None, :]) &
                                            (ip[:, None] > iq[None, :]))
        maskT[:, 512 * j:512 * (j + 1)] = np.where(m, -30000.0, 0.0)
    cvec = np.zeros((128, 8), np.float32)
    cvec[:, C_EXPB] = -12.0
    cvec[:, C_EPS] = EPS
    cvec[:, C_MAG] = 24576.0
    cvec[:, C_NMAG] = -24576.0
    return ident, identb, onest, eperm, maskT.astype(ml_dtypes.float8_e5m2), cvec


_NC_CACHE = [None]


def kernel(L0_attn_x, L0_attn_Wq, L0_attn_Wk, L0_attn_Wv, L0_attn_proj_w,
           L0_attn_proj_b, L0_ffn_weight, L0_ffn_bias):
    x = np.ascontiguousarray(np.asarray(L0_attn_x, dtype=np.float32))
    Wq = np.asarray(L0_attn_Wq, np.float32)
    Wk = np.asarray(L0_attn_Wk, np.float32)
    Wv = np.asarray(L0_attn_Wv, np.float32)
    pw = np.asarray(L0_attn_proj_w, np.float32)
    pb_ = np.asarray(L0_attn_proj_b, np.float32)
    fw = np.asarray(L0_ffn_weight, np.float32)
    fb = np.asarray(L0_ffn_bias, np.float32)

    wq2 = np.ascontiguousarray(Wq.transpose(1, 0, 2).reshape(1024, 1024))
    wk2 = np.ascontiguousarray(Wk.transpose(1, 0, 2).reshape(1024, 1024))
    wv2 = np.ascontiguousarray(Wv.transpose(1, 0, 2).reshape(1024, 1024))
    pw2 = np.ascontiguousarray(pw.T)
    fw2 = np.ascontiguousarray(fw.T)
    biasP = np.ascontiguousarray(np.broadcast_to(pb_, (128, 1024)))
    biasF = np.ascontiguousarray(np.broadcast_to(fb, (128, 1024)))
    ident, identb, onest, eperm, maskT, cvec = make_consts()

    if _NC_CACHE[0] is None:
        _NC_CACHE[0] = build_nc()
    nc = _NC_CACHE[0]

    in_maps = []
    for c in range(8):
        in_maps.append(dict(
            x_in=np.ascontiguousarray(x[c]), wq2=wq2, wk2=wk2, wv2=wv2, pw2=pw2,
            fw2=fw2, biasP=biasP, biasF=biasF, maskT=maskT, eperm=eperm,
            ident=ident, identb=identb, onest=onest, cvec=cvec))
    res = run_bass_kernel_spmd(nc, in_maps, core_ids=list(range(8)))
    return np.stack([r["out"] for r in res.results], axis=0)
